# version 12
# baseline (speedup 1.0000x reference)
import numpy as np
import ml_dtypes
import contextlib
import concourse.bass as bass
import concourse.mybir as mybir
from concourse.bass_utils import run_bass_kernel_spmd
from concourse.ap import AP

F32 = mybir.dt.float32
BF16 = mybir.dt.bfloat16
AF = mybir.ActivationFunctionType
ALU = mybir.AluOpType
AX = mybir.AxisListType

NCORES = 8
NSEQ = 4
NTOK = 2048
CT = 512
NCT = NTOK // CT
DM = 1024
NPC = DM // 128
HIDP = 2816
NFC = HIDP // 128
FGROUPS = [4, 4, 4, 4, 4, 2]

PH_LIMIT = None
BUILD_SEQS = NSEQ
A_SEC = None
A_CT = None
A_PAIR = None
F_SEC = None


def _host_consts():
    c = {}
    d = np.arange(64)
    half = d % 32
    inv_freq = 1.0 / (10000.0 ** (2.0 * half / 64.0))
    t = np.arange(NTOK, dtype=np.float64)
    freqs = t[None, :] * inv_freq[:, None]
    base = (2.0 * half + 25.6) / 89.6
    power = (t - NTOK // 2) / 256.0
    scale = base[:, None] ** power[None, :]
    sign = np.where(d < 32, -1.0, 1.0)[:, None]
    c["rotqc"] = np.tile((8.0 * np.cos(freqs) * scale), (2, 1)).astype(np.float32)
    c["rotqs"] = np.tile((sign * 8.0 * np.sin(freqs) * scale), (2, 1)).astype(np.float32)
    c["rotkc"] = np.tile((np.cos(freqs) / scale), (2, 1)).astype(np.float32)
    c["rotks"] = np.tile((sign * np.sin(freqs) / scale), (2, 1)).astype(np.float32)

    c["ident"] = np.eye(128, dtype=ml_dtypes.bfloat16)

    s = np.arange(128)[:, None]
    j = np.arange(128)[None, :]
    allowed = ((s // 64) == (j // 64)) & ((j % 64) <= (s % 64))
    amask = np.where(allowed, 0.0, -1e9).astype(np.float32)
    c["amsk4"] = np.tile(amask, (1, 4)).astype(ml_dtypes.bfloat16)
    mc = np.where(j <= s, 0.0, -1e9).astype(np.float32)
    mac = np.where(j >= s, 0.0, -1e9).astype(np.float32)
    c["mcb"] = mc.astype(ml_dtypes.bfloat16)
    c["macb"] = mac.astype(ml_dtypes.bfloat16)
    c["onesl"] = np.full((128, 128), 1.0 / 1024.0, dtype=ml_dtypes.bfloat16)
    bd = np.zeros((128, 128), dtype=np.float32)
    bd[:64, :64] = 1.0
    bd[64:, 64:] = 1.0
    c["bd64"] = bd.astype(ml_dtypes.bfloat16)
    cum = np.zeros((128, 36), dtype=np.float32)
    for tc in range(4):
        for cc in range(9):
            tglob = tc * 128 + np.arange(128)
            lim = 64 * cc if cc < 8 else CT
            cum[:, tc * 9 + cc] = np.where(tglob < lim, 1.0 / 64.0, 0.0)
    c["cumloc"] = cum.astype(ml_dtypes.bfloat16)
    sel = np.zeros((128, 4 * 64), dtype=np.float32)
    srow = np.arange(128)
    for p in range(4):
        for slot in range(8):
            for cc in range(8):
                sel[:, p * 64 + slot * 8 + cc] = np.where(cc == 2 * p + srow // 64,
                                                          0.5, 0.0)
    c["selt8"] = sel
    return c


def _host_pvec(inputs):
    cols, cmap = [], {}

    def add(key, vec):
        cmap[key] = len(cols)
        cols.append(np.asarray(vec, dtype=np.float32))

    qs = np.asarray(inputs["L0_ffn_q_scale"])
    ks = np.asarray(inputs["L0_ffn_k_scale"])
    lng = np.asarray(inputs["L0_ffn_ln_g"])
    lnb = np.asarray(inputs["L0_ffn_ln_b"])
    b1 = np.asarray(inputs["L0_ffn_b1"])
    b2 = np.asarray(inputs["L0_ffn_b2"])
    b1a = np.zeros((2, HIDP), np.float32)
    b1g = np.zeros((2, HIDP), np.float32)
    b1a[:, :2730] = b1[:, :2730]
    b1g[:, :2730] = b1[:, 2730:]
    for l in range(2):
        qt = np.tile(qs[l], 8)
        kt = np.tile(ks[l], 8)
        for pc in range(4):
            add(("qs", l, pc), qt[pc * 128:(pc + 1) * 128])
            add(("ks", l, pc), kt[pc * 128:(pc + 1) * 128])
        for pc in range(NPC):
            add(("lng", l, pc), lng[l, pc * 128:(pc + 1) * 128])
            add(("lnb", l, pc), lnb[l, pc * 128:(pc + 1) * 128])
            add(("b2", l, pc), b2[l, pc * 128:(pc + 1) * 128])
        for fc in range(NFC):
            add(("b1a", l, fc), b1a[l, fc * 128:(fc + 1) * 128])
            add(("b1g", l, fc), b1g[l, fc * 128:(fc + 1) * 128])
    add("eps5", np.full(128, 1e-5, np.float32))
    add("eps24", np.full(128, 1e-24, np.float32))
    return np.stack(cols, axis=1), cmap


def _host_weights(inputs):
    w = {}
    w["awqkv"] = np.ascontiguousarray(inputs["L0_attn_W_qkv"], dtype=np.float32)
    w["awout"] = np.ascontiguousarray(inputs["L0_attn_W_out"], dtype=np.float32)
    w["fwqkv"] = np.ascontiguousarray(inputs["L0_ffn_wqkv"], dtype=np.float32)
    w["fwo"] = np.ascontiguousarray(inputs["L0_ffn_wo"], dtype=np.float32)
    w1 = np.asarray(inputs["L0_ffn_w1"], dtype=np.float32)
    w1a = np.zeros((2, DM, HIDP), np.float32)
    w1g = np.zeros((2, DM, HIDP), np.float32)
    w1a[:, :, :2730] = w1[:, :, :2730]
    w1g[:, :, :2730] = w1[:, :, 2730:]
    w["fw1a"] = w1a
    w["fw1g"] = w1g
    w2 = np.asarray(inputs["L0_ffn_w2"], dtype=np.float32)
    w2p = np.zeros((2, HIDP, DM), np.float32)
    w2p[:, :2730, :] = w2
    w["fw2"] = w2p
    return w


class Ctr:
    def __init__(self, stack, nc, name):
        self.sem = stack.enter_context(nc.semaphore(name))
        self.n = 0

    def inc(self, instr, k=1):
        instr.then_inc(self.sem, k)
        self.n += k
        return self.n

    def wait(self, eng, n=None):
        eng.wait_ge(self.sem, self.n if n is None else n)


def build_nc(cmap_pv, npv):
    nc = bass.Bass("TRN2", target_bir_lowering=False)
    stack = contextlib.ExitStack()

    x_h = nc.dram_tensor("x", [NSEQ, DM, NTOK], F32, kind="ExternalInput").ap()
    out_h = nc.dram_tensor("out", [NSEQ, DM, NTOK], F32, kind="ExternalOutput").ap()
    w_h = {
        "awqkv": nc.dram_tensor("awqkv", [DM, 3072], F32, kind="ExternalInput").ap(),
        "awout": nc.dram_tensor("awout", [DM, DM], F32, kind="ExternalInput").ap(),
        "fwqkv": nc.dram_tensor("fwqkv", [2, DM, 1536], F32, kind="ExternalInput").ap(),
        "fwo": nc.dram_tensor("fwo", [2, 512, DM], F32, kind="ExternalInput").ap(),
        "fw1a": nc.dram_tensor("fw1a", [2, DM, HIDP], F32, kind="ExternalInput").ap(),
        "fw1g": nc.dram_tensor("fw1g", [2, DM, HIDP], F32, kind="ExternalInput").ap(),
        "fw2": nc.dram_tensor("fw2", [2, HIDP, DM], F32, kind="ExternalInput").ap(),
    }
    wb_h = {
        "awqkv": nc.dram_tensor("awqkv_b", [DM, 3072], BF16, kind="Internal").ap(),
        "awout": nc.dram_tensor("awout_b", [DM, DM], BF16, kind="Internal").ap(),
        "fwqkv": nc.dram_tensor("fwqkv_b", [2, DM, 1536], BF16, kind="Internal").ap(),
        "fwo": nc.dram_tensor("fwo_b", [2, 512, DM], BF16, kind="Internal").ap(),
        "fw1a": nc.dram_tensor("fw1a_b", [2, DM, HIDP], BF16, kind="Internal").ap(),
        "fw1g": nc.dram_tensor("fw1g_b", [2, DM, HIDP], BF16, kind="Internal").ap(),
        "fw2": nc.dram_tensor("fw2_b", [2, HIDP, DM], BF16, kind="Internal").ap(),
    }
    cst_h = {}
    for name, shape, dt in [
        ("rotqc", [128, NTOK], F32), ("rotqs", [128, NTOK], F32),
        ("rotkc", [128, NTOK], F32), ("rotks", [128, NTOK], F32),
        ("ident", [128, 128], BF16), ("amsk4", [128, 512], BF16),
        ("mcb", [128, 128], BF16),
        ("macb", [128, 128], BF16), ("onesl", [128, 128], BF16),
        ("bd64", [128, 128], BF16), ("cumloc", [128, 36], BF16),
        ("selt8", [128, 256], F32), ("pvec", [128, npv], F32),
    ]:
        cst_h[name] = nc.dram_tensor(name, shape, dt, kind="ExternalInput").ap()

    dbg_h = None
    if PH_LIMIT is not None:
        dbg_h = nc.dram_tensor("dbg", [DM, NTOK], F32, kind="ExternalOutput").ap()

    _names = [0]

    def mksb(st, name, shape, dt):
        _names[0] += 1
        return st.enter_context(nc.sbuf_tensor(f"{name}_{_names[0]}", shape, dt))

    psa = lambda name, shape, dt: stack.enter_context(nc.psum_tensor(name, shape, dt))

    XF = [mksb(stack, f"xf{pc}", [128, NTOK + 1], F32) for pc in range(NPC)]
    CST = {}
    for k in ("ident", "amsk4", "mcb", "macb", "onesl", "bd64", "cumloc"):
        CST[k] = mksb(stack, k, list(cst_h[k].tensor.shape), BF16)
    CST["selt8"] = mksb(stack, "selt8", [128, 256], F32)
    PVEC = mksb(stack, "pvec", [128, npv], F32)
    pv = lambda key: PVEC[:, cmap_pv[key]:cmap_pv[key] + 1]
    CARRYK = mksb(stack, "carryk", [128, 8], F32)
    CARRYV = mksb(stack, "carryv", [128, 8], F32)
    TT = mksb(stack, "tt", [128, 1024], F32)

    PB0 = psa("pb0", [128, 512], F32)
    PB1 = psa("pb1", [128, 512], F32)
    PS0 = psa("ps0", [128, 512], F32)
    PS1 = psa("ps1", [128, 512], F32)
    PO0 = psa("po0", [128, 512], F32)
    PO1 = psa("po1", [128, 512], F32)
    PGX = psa("pgx", [128, 512], F32)
    PTB = psa("ptb", [128, 1024], BF16)
    PW = [PB0, PB1, PS0, PS1, PO0, PO1, PGX]

    block = stack.enter_context(nc.Block())

    cLX = Ctr(stack, nc, "c_lx")
    cLW = Ctr(stack, nc, "c_lw")
    cLQ = Ctr(stack, nc, "c_lq")
    cLQv = Ctr(stack, nc, "c_lqv")
    cW2v = [Ctr(stack, nc, "c_w2va"), Ctr(stack, nc, "c_w2vb")]
    cW1p = [Ctr(stack, nc, "c_w1a"), Ctr(stack, nc, "c_w1b")]
    cW1g = [Ctr(stack, nc, "c_w1ga"), Ctr(stack, nc, "c_w1gb")]
    cW2p = [Ctr(stack, nc, "c_w2a"), Ctr(stack, nc, "c_w2b")]
    cW2g = [Ctr(stack, nc, "c_w2ga"), Ctr(stack, nc, "c_w2gb")]
    cOP = Ctr(stack, nc, "c_op")
    cCVp = [Ctr(stack, nc, "c_cv0"), Ctr(stack, nc, "c_cv1")]
    cSTc = [Ctr(stack, nc, "c_sc0"), Ctr(stack, nc, "c_sc1")]
    cSTp = [Ctr(stack, nc, "c_st0"), Ctr(stack, nc, "c_st1")]
    cPE = Ctr(stack, nc, "c_pe")
    cDV = Ctr(stack, nc, "c_dv")
    cAC = Ctr(stack, nc, "c_ac")
    cPL = Ctr(stack, nc, "c_pl")

    def phase_guard(sy):
        cPE.wait(sy)
        cDV.wait(sy)
        cAC.wait(sy)
        cPL.wait(sy)
        cSTp[0].wait(sy)
        cSTp[1].wait(sy)

    def conv_wait(sy):
        sy.wait_ge(cSTc[0].sem, conv_all_stored[0])
        sy.wait_ge(cSTc[1].sem, conv_all_stored[1])

    conv_chunks = []

    def add_conv(src, dst, nrows, ncols):
        for r0 in range(0, nrows, 128):
            conv_chunks.append((src, dst, r0, ncols))

    add_conv(w_h["awqkv"], wb_h["awqkv"], DM, 3072)
    add_conv(w_h["awout"], wb_h["awout"], DM, DM)
    for l in range(2):
        add_conv(w_h["fwqkv"][l], wb_h["fwqkv"][l], DM, 1536)
        add_conv(w_h["fwo"][l], wb_h["fwo"][l], 512, DM)
        add_conv(w_h["fw1a"][l], wb_h["fw1a"][l], DM, HIDP)
        add_conv(w_h["fw1g"][l], wb_h["fw1g"][l], DM, HIDP)
        add_conv(w_h["fw2"][l], wb_h["fw2"][l], HIDP, DM)

    cvst = contextlib.ExitStack()
    STG = mksb(cvst, "stg", [128, 2 * 5632], BF16)
    conv_store_done = [0] * len(conv_chunks)
    for i, (src, dst, r0, ncols) in enumerate(conv_chunks):
        par = i % 2

        @block.gpsimd
        def _(gp, i=i, src=src, r0=r0, ncols=ncols, par=par):
            if i >= 2:
                cSTc[par].wait(gp, conv_store_done[i - 2])
            cCVp[par].inc(gp.dma_start(out=STG[:, par * 5632:par * 5632 + ncols],
                                       in_=src[r0:r0 + 128, :]), 16)
        n_cv = cCVp[par].n

        @block.sync
        def _(sy, i=i, dst=dst, r0=r0, ncols=ncols, par=par, n_cv=n_cv):
            sy.wait_ge(cCVp[par].sem, n_cv)
            conv_store_done[i] = cSTc[par].inc(
                sy.dma_start(out=dst[r0:r0 + 128, :],
                             in_=STG[:, par * 5632:par * 5632 + ncols]), 16)
    conv_all_stored = (cSTc[0].n, cSTc[1].n)
    cvst.close()

    @block.vector
    def _(v):
        for pc in range(NPC):
            cDV.inc(v.memset(XF[pc][:, 0:1], 0.0))

    @block.sync
    def _(sy):
        for k in ("ident", "amsk4", "mcb", "macb", "onesl", "bd64",
                  "cumloc", "selt8"):
            cLW.inc(sy.dma_start(out=CST[k][:, :], in_=cst_h[k][:, :]), 16)
        cLW.inc(sy.dma_start(out=PVEC[:, :], in_=cst_h["pvec"][:, :]), 16)
    n_consts = cLW.n

    def emit_phase_a(seq):
        ph = contextlib.ExitStack()
        X32 = mksb(ph, "x32", [128, 4096], F32)
        XB = mksb(ph, "xb", [128, 1024], BF16)
        XSQ = mksb(ph, "xsq", [128, 1024], BF16)
        XNB = mksb(ph, "xnb", [128, 4096], BF16)
        QB = mksb(ph, "qb", [128, 4096], BF16)
        KB = mksb(ph, "kb", [128, 4096], BF16)
        VTM = mksb(ph, "vtm", [128, 4096], BF16)
        KTMt = mksb(ph, "ktmt", [128, 512], BF16)
        VT = mksb(ph, "vt", [128, 512], BF16)
        CK9 = mksb(ph, "ck9", [128, 18], F32)
        CKB = mksb(ph, "ckb", [128, 64], BF16)
        CVB = mksb(ph, "cvb", [128, 64], BF16)
        CVT = mksb(ph, "cvt", [8, 1024], BF16)
        OB = mksb(ph, "ob", [128, 4096], BF16)
        PEXP = mksb(ph, "pexp", [128, 1024], BF16)
        PTS = mksb(ph, "pts", [128, 1024], BF16)
        TST = mksb(ph, "tst", [8, 1024], BF16)
        SSUM = mksb(ph, "ssum", [128, 8], F32)
        SREC = mksb(ph, "srec", [128, 8], F32)
        EG = mksb(ph, "eg", [128, 64], F32)
        SG = mksb(ph, "sg", [128, 64], F32)
        TS_ = mksb(ph, "ts", [128, 64], BF16)
        MU2 = mksb(ph, "mu2", [128, 512], F32)
        VART = mksb(ph, "vart", [128, 512], F32)
        RREP = mksb(ph, "rrep", [128, 512], F32)
        AWQKV = mksb(ph, "awqkv", [128, 8 * 1536], BF16)
        AWOUT = mksb(ph, "awout", [128, 8 * 1024], BF16)

        @block.sync
        def _(sy):
            phase_guard(sy)
            conv_wait(sy)
            src = AP(wb_h["awout"].tensor, 0,
                     [[1024, 128], [128 * 1024, NPC], [1, 1024]])
            cLW.inc(sy.dma_start(
                out=AWOUT[:, :].rearrange("p (a b) -> p a b", a=NPC),
                in_=src), 16)
        n_aw = cLW.n

        @block.vector
        def _(v):
            v.memset(CARRYK[:, :], 0.0)
            cDV.inc(v.memset(CARRYV[:, :], 0.0))

        for ct in range(NCT if A_CT is None else A_CT):
            t0 = ct * CT

            @block.sync
            def _(sy, seq=seq, t0=t0, ct=ct):
                cDV.wait(sy)
                cSTp[(ct - 1) % 2].wait(sy)
                src = AP(x_h.tensor, seq * DM * NTOK + t0,
                         [[NTOK, 128], [128 * NTOK, NPC], [1, CT]])
                cLX.inc(sy.dma_start(out=X32[:, :], in_=src), 16)
            n_x = cLX.n

            xb_pe = [0, 0]
            xsq_pe = [0, 0]
            xb_ac = [0, 0]
            xsq_dv = [0, 0]
            for pc in range(NPC):
                par = pc % 2

                @block.gpsimd
                def _(g, pc=pc, par=par, n_x=n_x, mufree=xb_pe[par],
                      sqfree=xsq_dv[par]):
                    if pc == 0:
                        g.wait_ge(cLX.sem, n_x)
                    if pc >= 2:
                        cPE.wait(g, mufree)
                        cDV.wait(g, sqfree)
                    cPL.inc(g.tensor_copy(XB[:, par * 512:(par + 1) * 512],
                                          X32[:, pc * 512:(pc + 1) * 512]))
                xb_ac[par] = cPL.n

                @block.tensor
                def _(te, pc=pc, par=par, nxb=xb_ac[par]):
                    cPL.wait(te, nxb)
                    if pc == 0:
                        cLW.wait(te)
                    cPE.inc(te.matmul(PB0[:, :], CST["onesl"][:, :],
                                      XB[:, par * 512:(par + 1) * 512],
                                      start=(pc == 0), stop=(pc == NPC - 1)))
                xb_pe[par] = cPE.n
                n_mu = cPE.n

                @block.vector
                def _(v, pc=pc, par=par, nxb=xb_ac[par], sqf=xsq_pe[par]):
                    cPL.wait(v, nxb)
                    if pc >= 2:
                        cPE.wait(v, sqf)
                    cDV.inc(v.tensor_mul(XSQ[:, par * 512:(par + 1) * 512],
                                         XB[:, par * 512:(par + 1) * 512],
                                         XB[:, par * 512:(par + 1) * 512]))
                xsq_dv[par] = cDV.n

                @block.tensor
                def _(te, pc=pc, par=par, nsq=xsq_dv[par]):
                    cDV.wait(te, nsq)
                    cPE.inc(te.matmul(PB1[:, :], CST["onesl"][:, :],
                                      XSQ[:, par * 512:(par + 1) * 512],
                                      start=(pc == 0), stop=(pc == NPC - 1)))
                xsq_pe[par] = cPE.n
            n_msq = cPE.n

            @block.scalar
            def _(s, n_mu=n_mu):
                cPE.wait(s, n_mu)
                cAC.inc(s.activation(MU2[:, :], PB0[:, :], AF.Square))

            @block.vector
            def _(v, n_msq=n_msq):
                cAC.wait(v)
                cPE.wait(v, n_msq)
                cDV.inc(v.tensor_sub(VART[:, :], PB1[:, :], MU2[:, :]))

            @block.scalar
            def _(s):
                cDV.wait(s)
                cAC.inc(s.activation(VART[:, :], VART[:, :], AF.Sqrt,
                                     bias=pv("eps5")))
            n_srt = cAC.n

            @block.vector
            def _(v, n_srt=n_srt):
                cAC.wait(v, n_srt)
                cDV.inc(v.reciprocal(RREP[:, :], VART[:, :]))
            n_rrep = cDV.n

            g_pl = [0, 0]
            for pc in range(NPC):
                par = pc % 2

                @block.vector
                def _(v, pc=pc, par=par, n_rrep=n_rrep, ttfree=g_pl[par]):
                    if pc == 0:
                        cDV.wait(v, n_rrep)
                    if pc >= 2:
                        cPL.wait(v, ttfree)
                    cDV.inc(v.tensor_sub(TT[:, par * 512:(par + 1) * 512],
                                         X32[:, pc * 512:(pc + 1) * 512], PB0[:, :]))

                @block.gpsimd
                def _(g, pc=pc, par=par, t0=t0):
                    cDV.wait(g)
                    cPL.inc(g.tensor_tensor(XF[pc][:, 1 + t0:1 + t0 + CT],
                                            TT[:, par * 512:(par + 1) * 512],
                                            RREP[:, :], ALU.mult))
                g_pl[par] = cPL.n

                @block.gpsimd
                def _(g, pc=pc, t0=t0):
                    cPL.wait(g)
                    cPL.inc(g.tensor_copy(XNB[:, pc * 512:(pc + 1) * 512],
                                          XF[pc][:, 1 + t0:1 + t0 + CT]))
            n_xnb = cPL.n

            if A_SEC is not None and A_SEC <= 1:
                continue
            qkv_ac = [0, 0]
            for halfq in range(2):
                @block.sync
                def _(sy, halfq=halfq):
                    cPE.wait(sy)
                    src = AP(wb_h["awqkv"].tensor, halfq * 1536,
                             [[3072, 128], [128 * 3072, 4], [1, 1536]])
                    cLQ.inc(sy.dma_start(
                        out=AWQKV[:, 0:4 * 1536].rearrange("p (a b) -> p a b",
                                                           a=4),
                        in_=src), 16)

                @block.scalar
                def _(v, halfq=halfq):
                    cPE.wait(v)
                    src = AP(wb_h["awqkv"].tensor, 4 * 128 * 3072 + halfq * 1536,
                             [[3072, 128], [128 * 3072, 4], [1, 1536]])
                    cLQv.inc(v.dma_start(
                        out=AWQKV[:, 4 * 1536:8 * 1536]
                        .rearrange("p (a b) -> p a b", a=4),
                        in_=src), 16)
                n_awq = cLQ.n
                n_awqv = cLQv.n

                for ofl in range(12):
                    ofc = halfq * 12 + ofl
                    pbank = PB0 if ofc % 2 == 0 else PB1

                    @block.tensor
                    def _(te, ofl=ofl, ofc=ofc, pbank=pbank, n_awq=n_awq,
                          bankfree=qkv_ac[ofc % 2], n_awqv=n_awqv):
                        if ofl == 0:
                            cPL.wait(te, n_xnb)
                            te.wait_ge(cLQ.sem, n_awq)
                            te.wait_ge(cLQv.sem, n_awqv)
                            cLW.wait(te)
                        cDV.wait(te, bankfree)
                        for kc in range(NPC):
                            cPE.inc(te.matmul(
                                pbank[:, :],
                                AWQKV[:, kc * 1536 + ofl * 128:kc * 1536 + ofl * 128 + 128],
                                XNB[:, kc * 512:(kc + 1) * 512],
                                start=(kc == 0), stop=(kc == NPC - 1)))

                    @block.vector
                    def _(v, ofc=ofc, pbank=pbank):
                        cPE.wait(v)
                        if ofc < 8:
                            cDV.inc(v.tensor_scalar(
                                QB[:, ofc * 512:(ofc + 1) * 512],
                                pbank[:, :], 0.125, None, ALU.mult))
                        elif ofc < 16:
                            cDV.inc(v.tensor_copy(
                                KB[:, (ofc - 8) * 512:(ofc - 7) * 512],
                                pbank[:, :]))
                        else:
                            cDV.inc(v.tensor_copy(VT[:, :], pbank[:, :]))
                    qkv_ac[ofc % 2] = cDV.n

                    if ofc >= 16:
                        pcv = ofc - 16

                        @block.tensor
                        def _(te):
                            cDV.wait(te)
                            for tb in range(4):
                                cPE.inc(te.transpose(PTB[:, tb * 128:(tb + 1) * 128],
                                                     VT[:, tb * 128:(tb + 1) * 128],
                                                     CST["ident"][:, :]))

                        @block.vector
                        def _(v, pcv=pcv):
                            cPE.wait(v)
                            base = VTM[:, :]
                            dst = AP(base.tensor, base.offset + pcv * 128,
                                     [[4096, 128], [1024, 4], [1, 128]])
                            cDV.inc(v.tensor_copy(
                                dst, PTB[:, 0:512].rearrange("p (a b) -> p a b", a=4)))

            if A_SEC is not None and A_SEC <= 2:
                continue
            for pc in range(NPC):
                @block.tensor
                def _(te, pc=pc):
                    cDV.wait(te)
                    for tb in range(4):
                        cPE.inc(te.transpose(
                            PTB[:, tb * 128:(tb + 1) * 128],
                            KB[:, pc * 512 + tb * 128:pc * 512 + (tb + 1) * 128],
                            CST["ident"][:, :]))

                @block.vector
                def _(v):
                    cPE.wait(v)
                    cDV.inc(v.tensor_copy(KTMt[:, :], PTB[:, 0:512]))

                @block.tensor
                def _(te, pc=pc):
                    cDV.wait(te)
                    for tb in range(4):
                        cPE.inc(te.matmul(PGX[:, pc * 9:pc * 9 + 9],
                                          KTMt[:, tb * 128:(tb + 1) * 128],
                                          CST["cumloc"][:, tb * 9:(tb + 1) * 9],
                                          start=(tb == 0), stop=(tb == 3)))
                    for tb in range(4):
                        cPE.inc(te.matmul(
                            PGX[:, 128 + pc * 9:128 + pc * 9 + 9],
                            VTM[:, tb * 1024 + pc * 128:tb * 1024 + (pc + 1) * 128],
                            CST["cumloc"][:, tb * 9:(tb + 1) * 9],
                            start=(tb == 0), stop=(tb == 3)))
            n_ckmm = cPE.n

            for pc in range(NPC):
                par = pc % 2

                @block.vector
                def _(v, pc=pc, par=par, n_ckmm=n_ckmm):
                    if pc == 0:
                        cPE.wait(v, n_ckmm)
                    cDV.wait(v)
                    cDV.inc(v.tensor_scalar(CK9[:, par * 9:(par + 1) * 9],
                                            PGX[:, pc * 9:pc * 9 + 9],
                                            CARRYK[:, pc:pc + 1], None, ALU.add))
                    cDV.wait(v)
                    cDV.inc(v.tensor_copy(CARRYK[:, pc:pc + 1],
                                          CK9[:, par * 9 + 8:par * 9 + 9]))
                    cDV.inc(v.tensor_copy(CKB[:, pc * 8:(pc + 1) * 8],
                                          CK9[:, par * 9:par * 9 + 8]))
            for pc in range(NPC):
                par = pc % 2

                @block.vector
                def _(v, pc=pc, par=par):
                    cDV.wait(v)
                    cDV.inc(v.tensor_scalar(CK9[:, par * 9:(par + 1) * 9],
                                            PGX[:, 128 + pc * 9:128 + pc * 9 + 9],
                                            CARRYV[:, pc:pc + 1], None, ALU.add))
                    cDV.wait(v)
                    cDV.inc(v.tensor_copy(CARRYV[:, pc:pc + 1],
                                          CK9[:, par * 9 + 8:par * 9 + 9]))
                    cDV.inc(v.tensor_copy(CVB[:, pc * 8:(pc + 1) * 8],
                                          CK9[:, par * 9:par * 9 + 8]))
            n_ckcv = cDV.n

            @block.tensor
            def _(te, n_ckcv=n_ckcv):
                cDV.wait(te, n_ckcv)
                cDV.wait(te)
                for pc in range(NPC):
                    cPE.inc(te.transpose(PTB[0:8, pc * 128:(pc + 1) * 128],
                                         CVB[:, pc * 8:(pc + 1) * 8],
                                         CST["ident"][:, :]))

            @block.vector
            def _(v):
                cPE.wait(v)
                cDV.inc(v.tensor_copy(CVT[0:8, :], PTB[0:8, 0:1024]))

            if A_SEC is not None and A_SEC <= 3:
                continue
            for p in range(4):
                for half in range(2):
                    pob = PO0 if half == 0 else PO1

                    @block.tensor
                    def _(te, p=p, half=half):
                        cAC.wait(te)
                        cDV.wait(te)
                        for hh in range(2):
                            cPE.inc(te.transpose(PTB[:, 0:128], XNB[:, 0:128],
                                                 CST["ident"][:, :]))
                            for hpl in range(4):
                                hp = half * 4 + hpl
                                bank = PS0 if hpl < 2 else PS1
                                slot = (hpl % 2) * 2 + hh
                                qsl = QB[hh * 64:hh * 64 + 64,
                                         hp * 512 + p * 128:hp * 512 + (p + 1) * 128]
                                ksl = KB[hh * 64:hh * 64 + 64,
                                         hp * 512 + p * 128:hp * 512 + (p + 1) * 128]
                                cPE.inc(te.matmul(
                                    bank[:, slot * 128:(slot + 1) * 128],
                                    qsl, ksl, start=True, stop=True))
                    n_s = cPE.n
                    if A_PAIR is not None and A_PAIR <= 1:
                        continue

                    @block.vector
                    def _(v, n_s=n_s):
                        cPE.wait(v, n_s)
                        cDV.inc(v.tensor_tensor(PS0[:, :], PS0[:, :],
                                                CST["amsk4"][:, :], ALU.add))
                        cDV.inc(v.tensor_tensor(PS1[:, :], PS1[:, :],
                                                CST["amsk4"][:, :], ALU.add))
                    n_madd = cDV.n

                    @block.scalar
                    def _(s, n_madd=n_madd):
                        cDV.wait(s, n_madd)
                        cAC.inc(s.activation(PEXP[:, 0:512], PS0[:, :], AF.Exp))
                        cAC.inc(s.activation(PEXP[:, 512:1024], PS1[:, :],
                                             AF.Exp))
                    n_exp = cAC.n
                    if A_PAIR is not None and A_PAIR <= 2:
                        continue

                    @block.vector
                    def _(v, n_exp=n_exp):
                        cAC.wait(v, n_exp)
                        cDV.inc(v.reduce_sum(
                            SSUM[:, 0:8],
                            PEXP[:, :].rearrange("p (c j) -> p c j", c=8),
                            axis=AX.X))
                        cDV.wait(v)
                        cDV.inc(v.reciprocal(SREC[:, 0:8], SSUM[:, 0:8]))
                    n_rec = cDV.n
                    if A_PAIR is not None and A_PAIR <= 3:
                        continue

                    @block.vector
                    def _(v, n_rec=n_rec):
                        cDV.wait(v, n_rec)
                        for sl in range(8):
                            cDV.inc(v.tensor_scalar(
                                PEXP[:, sl * 128:(sl + 1) * 128],
                                PEXP[:, sl * 128:(sl + 1) * 128],
                                SREC[:, sl:sl + 1], None, ALU.mult))
                    n_norm = cDV.n
                    if A_PAIR is not None and A_PAIR <= 4:
                        continue

                    @block.tensor
                    def _(te, n_norm=n_norm):
                        cDV.wait(te, n_norm)
                        for sl in range(8):
                            cPE.inc(te.transpose(PTB[:, sl * 128:(sl + 1) * 128],
                                                 PEXP[:, sl * 128:(sl + 1) * 128],
                                                 CST["ident"][:, :]))

                    @block.vector
                    def _(v):
                        cPE.wait(v)
                        cDV.inc(v.tensor_copy(PTS[:, :], PTB[:, :]))
                    n_pts = cDV.n
                    if A_PAIR is not None and A_PAIR <= 5:
                        continue

                    @block.tensor
                    def _(te, p=p, half=half):
                        cDV.wait(te)
                        for hh in range(2):
                            cPE.inc(te.transpose(PTB[:, 0:128], XNB[:, 0:128],
                                                 CST["ident"][:, :]))
                            for hpl in range(4):
                                hp = half * 4 + hpl
                                slot = hpl * 2 + hh
                                qsl = QB[hh * 64:hh * 64 + 64,
                                         hp * 512 + p * 128:hp * 512 + (p + 1) * 128]
                                cksl = CKB[hh * 64:hh * 64 + 64, hp * 8:hp * 8 + 8]
                                cPE.inc(te.matmul(PGX[:, slot * 8:(slot + 1) * 8],
                                                  qsl, cksl, start=True,
                                                  stop=True))
                    n_g = cPE.n

                    @block.scalar
                    def _(s, n_g=n_g):
                        cPE.wait(s, n_g)
                        cAC.inc(s.activation(EG[:, :], PGX[:, 0:64], AF.Exp,
                                             scale=-1.0))

                    @block.vector
                    def _(v):
                        cAC.wait(v)
                        cDV.inc(v.tensor_scalar(EG[:, :], EG[:, :], 1.0, None,
                                                ALU.add))
                        cDV.wait(v)
                        cDV.inc(v.reciprocal(SG[:, :], EG[:, :]))

                    @block.gpsimd
                    def _(g, p=p):
                        cDV.wait(g)
                        cPL.inc(g.tensor_tensor(TS_[:, :], SG[:, :],
                                                CST["selt8"][:, p * 64:(p + 1) * 64],
                                                ALU.mult))
                    n_ts = cPL.n

                    @block.tensor
                    def _(te, n_ts=n_ts):
                        cPL.wait(te, n_ts)
                        cDV.wait(te)
                        for sl in range(8):
                            cPE.inc(te.transpose(PTB[0:8, sl * 128:(sl + 1) * 128],
                                                 TS_[:, sl * 8:(sl + 1) * 8],
                                                 CST["ident"][:, :]))

                    @block.vector
                    def _(v):
                        cPE.wait(v)
                        cDV.inc(v.tensor_copy(TST[0:8, :], PTB[0:8, :]))
                    n_tst = cDV.n
                    if A_PAIR is not None and A_PAIR <= 6:
                        continue

                    @block.tensor
                    def _(te, p=p, half=half, pob=pob, n_tst=n_tst, n_pts=n_pts):
                        cDV.wait(te, n_tst)
                        for hpl in range(4):
                            hp = half * 4 + hpl
                            for hh in range(2):
                                slot = hpl * 2 + hh
                                vsl = VTM[:, p * 1024 + hp * 128 + hh * 64:
                                          p * 1024 + hp * 128 + hh * 64 + 64]
                                cPE.inc(te.matmul(
                                    pob[hh * 64:hh * 64 + 64,
                                        hpl * 128:(hpl + 1) * 128],
                                    vsl, PTS[:, slot * 128:(slot + 1) * 128],
                                    start=True, stop=False))
                                cvsl = CVT[0:8, hp * 128 + hh * 64:
                                           hp * 128 + hh * 64 + 64]
                                cPE.inc(te.matmul(
                                    pob[hh * 64:hh * 64 + 64,
                                        hpl * 128:(hpl + 1) * 128],
                                    cvsl, TST[0:8, slot * 128:(slot + 1) * 128],
                                    start=False, stop=True))
                    n_cross = cPE.n

                    @block.vector
                    def _(v, p=p, half=half, pob=pob, n_cross=n_cross):
                        cPE.wait(v, n_cross)
                        base = OB[:, :]
                        dst = AP(base.tensor, base.offset + half * 4 * 512 + p * 128,
                                 [[4096, 128], [512, 4], [1, 128]])
                        cDV.inc(v.tensor_copy(
                            dst, pob[:, :].rearrange("p (a b) -> p a b", a=4)))

            if A_SEC is not None and A_SEC <= 4:
                continue
            wo_dv = [0, 0]
            for ofc in range(NPC):
                pbank = PB0 if ofc % 2 == 0 else PB1

                @block.tensor
                def _(te, ofc=ofc, pbank=pbank, bankfree=wo_dv[ofc % 2]):
                    if ofc < 2:
                        cAC.wait(te)
                        cDV.wait(te)
                    else:
                        cDV.wait(te, bankfree)
                    for kc in range(NPC):
                        cPE.inc(te.matmul(
                            pbank[:, :],
                            AWOUT[:, kc * 1024 + ofc * 128:kc * 1024 + ofc * 128 + 128],
                            OB[:, kc * 512:(kc + 1) * 512],
                            start=(kc == 0), stop=(kc == NPC - 1)))

                @block.vector
                def _(v, ofc=ofc, pbank=pbank):
                    cPE.wait(v)
                    cDV.inc(v.tensor_add(X32[:, ofc * 512:(ofc + 1) * 512],
                                         pbank[:, :],
                                         X32[:, ofc * 512:(ofc + 1) * 512]))
                wo_dv[ofc % 2] = cDV.n
            n_outt = cDV.n

            @block.sync
            def _(sy, seq=seq, t0=t0, n_outt=n_outt, ct=ct):
                cDV.wait(sy, n_outt)
                dst = AP(out_h.tensor, seq * DM * NTOK + t0,
                         [[NTOK, 128], [128 * NTOK, NPC], [1, CT]])
                cSTp[ct % 2].inc(sy.dma_start(out=dst, in_=X32[:, :]), 16)

        ph.close()

    def emit_attn_sweep(seq, l, MUS, RS):
        ph = contextlib.ExitStack()
        ROT = {k: mksb(ph, k, [128, NTOK], F32)
               for k in ("rotqc", "rotqs", "rotkc", "rotks")}
        FWQKV = mksb(ph, "fwqkv", [128, 8 * 768], BF16)
        FWO = mksb(ph, "fwo", [128, 4 * 1024], BF16)
        XFB = mksb(ph, "xfb", [128, 8 * 513], BF16)
        XSQ2 = mksb(ph, "xsq2", [128, 1026], BF16)
        Q32t = mksb(ph, "q32t", [128, 1024], F32)
        QSQt = mksb(ph, "qsqt", [128, 1024], BF16)
        aq = {"bf_dv": [0, 0], "bf_ac": [0, 0], "qsq_dv": [0, 0],
              "nn2": [0, 0], "ln_ac": [0, 0], "t3_dv": [0, 0],
              "exp_ac": [0, 0], "rot_pl": 0}
        la = {"mx": [0, 0, 0], "exp": [0, 0, 0], "pts2": [0, 0, 0], "obf": 0}
        RNt = mksb(ph, "rnt", [128, 1024], F32)
        T3 = mksb(ph, "t3", [128, 512], F32)
        R1 = mksb(ph, "r1", [128, 512], F32)
        R2 = mksb(ph, "r2", [128, 512], F32)
        RH = mksb(ph, "rh", [128, 512], F32)
        QTB = mksb(ph, "qtb", [128, 4 * 512], BF16)
        KTB = mksb(ph, "ktb", [128, 4 * 1024], BF16)
        VTMR = mksb(ph, "vtmr", [128, 8 * 512], BF16)
        VT = mksb(ph, "vt", [128, 512], BF16)
        MN1 = mksb(ph, "mn1", [128, 12], F32)
        SS2 = mksb(ph, "ss2", [128, 12], F32)
        PE1 = mksb(ph, "pe1", [128, 1536], BF16)
        PE2 = mksb(ph, "pe2", [128, 384], BF16)
        PTS2 = mksb(ph, "pts2", [128, 1920], BF16)
        OBF = mksb(ph, "obf", [128, 4 * 512], BF16)
        MU2 = mksb(ph, "mu2", [128, 512], F32)
        VART = mksb(ph, "vart", [128, 512], F32)
        XFBND = mksb(ph, "xfbnd", [128, 16], BF16)

        @block.sync
        def _(sy, l=l):
            phase_guard(sy)
            conv_wait(sy)
            for k in ("rotqc", "rotqs", "rotkc", "rotks"):
                cLW.inc(sy.dma_start(out=ROT[k][:, :], in_=cst_h[k][:, :]), 16)
            src = AP(wb_h["fwo"].tensor, l * 512 * 1024,
                     [[1024, 128], [128 * 1024, 4], [1, 1024]])
            cLW.inc(sy.dma_start(
                out=FWO[:, :].rearrange("p (a b) -> p a b", a=4),
                in_=src), 16)
        n_fw = cLW.n

        for ct in range(NCT):
            t0 = ct * CT
            h0 = ct % 2

            @block.gpsimd
            def _(g, t0=t0, ct=ct):
                cPE.wait(g)
                cDV.wait(g)
                cPL.wait(g)
                for pc in range(NPC):
                    cPL.inc(g.tensor_copy(XFB[:, pc * 513:(pc + 1) * 513],
                                          XF[pc][:, t0:t0 + 513]))
                cPL.wait(g)
                sv, rd = (ct % 2) * 8, ((ct - 1) % 2) * 8
                for pc in range(NPC):
                    cPL.inc(g.tensor_copy(XFBND[:, sv + pc:sv + pc + 1],
                                          XFB[:, pc * 513 + 512:pc * 513 + 513]))
                if ct > 0:
                    for pc in range(NPC):
                        cPL.inc(g.tensor_copy(XFB[:, pc * 513:pc * 513 + 1],
                                              XFBND[:, rd + pc:rd + pc + 1]))
            n_xfb = cPL.n

            pending_post = [None]
            for halfq in range(2):
                @block.sync
                def _(sy, l=l, halfq=halfq):
                    cPE.wait(sy)
                    src = AP(wb_h["fwqkv"].tensor, l * DM * 1536 + halfq * 768,
                             [[1536, 128], [128 * 1536, 4], [1, 768]])
                    cLQ.inc(sy.dma_start(
                        out=FWQKV[:, 0:4 * 768].rearrange("p (a b) -> p a b",
                                                          a=4),
                        in_=src), 16)

                @block.scalar
                def _(v, l=l, halfq=halfq):
                    cPE.wait(v)
                    src = AP(wb_h["fwqkv"].tensor,
                             l * DM * 1536 + 4 * 128 * 1536 + halfq * 768,
                             [[1536, 128], [128 * 1536, 4], [1, 768]])
                    cLQv.inc(v.dma_start(
                        out=FWQKV[:, 4 * 768:8 * 768]
                        .rearrange("p (a b) -> p a b", a=4),
                        in_=src), 16)
                n_fq = cLQ.n
                n_fqv = cLQv.n

                for ofl in range(6):
                    ofc = halfq * 6 + ofl
                    par2 = ofc % 2
                    pbank = PB0 if ofc % 2 == 0 else PB1

                    @block.tensor
                    def _(te, ofl=ofl, pbank=pbank, n_fq=n_fq, n_xfb=n_xfb,
                          bfdv=aq["bf_dv"][par2], bfac=aq["bf_ac"][par2],
                          n_fqv=n_fqv):
                        if ofl == 0:
                            cPL.wait(te, n_xfb)
                            te.wait_ge(cLQ.sem, n_fq)
                            te.wait_ge(cLQv.sem, n_fqv)
                            cLW.wait(te)
                        cAC.wait(te, bfac)
                        cDV.wait(te, bfdv)
                        for kc in range(NPC):
                            off = 1 if kc < 4 else 0
                            cPE.inc(te.matmul(
                                pbank[:, :],
                                FWQKV[:, kc * 768 + ofl * 128:kc * 768 + ofl * 128 + 128],
                                XFB[:, kc * 513 + off:kc * 513 + off + 512],
                                start=(kc == 0), stop=(kc == NPC - 1)))
                    n_mm = cPE.n

                    if ofc < 8:
                        isq = ofc < 4
                        pc4 = ofc % 4

                        @block.vector
                        def _(v, pbank=pbank, par2=par2, n_mm=n_mm):
                            cPE.wait(v, n_mm)
                            cDV.inc(v.tensor_copy(
                                Q32t[:, par2 * 512:(par2 + 1) * 512], pbank[:, :]))
                            cDV.wait(v)
                            cDV.inc(v.tensor_mul(
                                QSQt[:, par2 * 512:(par2 + 1) * 512],
                                Q32t[:, par2 * 512:(par2 + 1) * 512],
                                Q32t[:, par2 * 512:(par2 + 1) * 512]))
                        aq["bf_dv"][par2] = cDV.n
                        aq["qsq_dv"][par2] = cDV.n

                        def post(ofc=ofc, isq=isq, pc4=pc4, par2=par2,
                                 n_qsq=aq["qsq_dv"][par2], t0=t0, h0=h0, l=l):
                            psb = PS0 if par2 == 0 else PS1

                            @block.tensor
                            def _(te, lnfree=aq["ln_ac"][par2]):
                                cDV.wait(te, n_qsq)
                                cAC.wait(te, lnfree)
                                cPE.inc(te.matmul(
                                    psb[:, :], CST["bd64"][:, :],
                                    QSQt[:, par2 * 512:(par2 + 1) * 512],
                                    start=True, stop=True))
                            aq["nn2"][par2] = cPE.n
                            n_n2 = cPE.n

                            @block.scalar
                            def _(s, n_n2=n_n2, rntfree=aq["t3_dv"][par2]):
                                cPE.wait(s, n_n2)
                                cDV.wait(s, rntfree)
                                cAC.inc(s.activation(
                                    RNt[:, par2 * 512:(par2 + 1) * 512],
                                    psb[:, :], AF.Sqrt, bias=pv("eps24")))
                            aq["ln_ac"][par2] = cAC.n
                            aq["exp_ac"][par2] = cAC.n

                            @block.vector
                            def _(v, nexp=aq["exp_ac"][par2],
                                  rotfree=aq["rot_pl"]):
                                cAC.wait(v, nexp)
                                cPL.wait(v, rotfree)
                                cDV.wait(v)
                                key = ("qs" if isq else "ks", l, pc4)
                                cDV.inc(v.reciprocal(
                                    RNt[:, par2 * 512:(par2 + 1) * 512],
                                    RNt[:, par2 * 512:(par2 + 1) * 512]))
                                cDV.inc(v.tensor_scalar(
                                    T3[:, :], Q32t[:, par2 * 512:(par2 + 1) * 512],
                                    pv(key), None, ALU.mult))
                                cDV.wait(v)
                                cDV.inc(v.tensor_tensor(
                                    T3[:, :], T3[:, :],
                                    RNt[:, par2 * 512:(par2 + 1) * 512],
                                    ALU.mult))
                            n_t3 = cDV.n
                            aq["t3_dv"][par2] = n_t3

                            cs = ROT["rotqc" if isq else "rotkc"]
                            ss = ROT["rotqs" if isq else "rotks"]

                            @block.gpsimd
                            def _(g, cs=cs, n_t3=n_t3):
                                cDV.wait(g, n_t3)
                                cPL.inc(g.tensor_copy(RH[0:32, :], T3[32:64, :]))
                                cPL.inc(g.tensor_copy(RH[64:96, :], T3[96:128, :]))
                                cPL.inc(g.tensor_tensor(R1[:, :], T3[:, :],
                                                        cs[:, t0:t0 + CT],
                                                        ALU.mult))
                            aq["rot_pl"] = cPL.n

                            @block.vector
                            def _(v):
                                cDV.wait(v)
                                cDV.inc(v.tensor_copy(RH[32:64, :], T3[0:32, :]))
                                cDV.inc(v.tensor_copy(RH[96:128, :], T3[64:96, :]))

                            @block.vector
                            def _(v, rotn=aq["rot_pl"]):
                                cPL.wait(v, rotn)
                                cDV.wait(v)
                                cDV.inc(v.tensor_tensor(R2[:, :], RH[:, :],
                                                        ss[:, t0:t0 + CT],
                                                        ALU.mult))

                            @block.vector
                            def _(v):
                                cDV.wait(v)
                                if isq:
                                    dst = QTB[:, pc4 * 512:(pc4 + 1) * 512]
                                else:
                                    dst = KTB[:, pc4 * 1024 + h0 * 512:
                                              pc4 * 1024 + h0 * 512 + 512]
                                cDV.inc(v.tensor_add(dst, R1[:, :], R2[:, :]))

                        if pending_post[0] is not None:
                            pending_post[0]()
                        pending_post[0] = post
                    else:
                        pcv = ofc - 8
                        if pending_post[0] is not None:
                            pending_post[0]()
                            pending_post[0] = None

                        @block.vector
                        def _(v, pbank=pbank, n_mm=n_mm):
                            cPE.wait(v, n_mm)
                            cDV.inc(v.tensor_copy(VT[:, :], pbank[:, :]))
                        aq["bf_dv"][ofc % 2] = cDV.n

                        @block.tensor
                        def _(te):
                            cDV.wait(te)
                            for tb in range(4):
                                cPE.inc(te.transpose(
                                    PTB[:, tb * 128:(tb + 1) * 128],
                                    VT[:, tb * 128:(tb + 1) * 128],
                                    CST["ident"][:, :]))

                        @block.vector
                        def _(v, pcv=pcv, h0=h0):
                            cPE.wait(v)
                            base = VTMR[:, :]
                            dst = AP(base.tensor,
                                     base.offset + h0 * 4 * 512 + pcv * 128,
                                     [[8 * 512, 128], [512, 4], [1, 128]])
                            cDV.inc(v.tensor_copy(
                                dst, PTB[:, 0:512].rearrange("p (a b) -> p a b",
                                                             a=4)))

            la_q = []
            for h in range(8):
                hp, hh = h // 2, h % 2
                for qc in range(4):
                    u = h * 4 + qc
                    pu = u % 3
                    lo = qc if ct > 0 else max(qc, 4)
                    kbs = list(range(lo, qc + 5))
                    nlive = len(kbs)
                    n_a = min(nlive, 4)
                    use_b = nlive > 4
                    bankA = (PS0, PB0, PO1)[pu]
                    bankB = (PS1, PB1, PGX)[pu]

                    if len(la_q) >= 2:
                        la_q[0]("head")

                    @block.tensor
                    def _(te, hp=hp, hh=hh, qc=qc, lo=lo, kbs=kbs, ct=ct, u=u,
                          bankA=bankA, bankB=bankB,
                          mxfree=la["mx"][pu], expfree=la["exp"][pu]):
                        if u < 3:
                            cAC.wait(te)
                            cDV.wait(te)
                        else:
                            cAC.wait(te, expfree)
                            cDV.wait(te, mxfree)
                        for kb in kbs:
                            sl = kb - lo
                            bank = bankA if sl < 4 else bankB
                            col = (sl % 4) * 128
                            half_kb = ((ct - 1) % 2) if kb < 4 else (ct % 2)
                            ksl = KTB[hh * 64:hh * 64 + 64,
                                      hp * 1024 + half_kb * 512 + (kb % 4) * 128:
                                      hp * 1024 + half_kb * 512 + (kb % 4) * 128 + 128]
                            qsl = QTB[hh * 64:hh * 64 + 64,
                                      hp * 512 + qc * 128:hp * 512 + (qc + 1) * 128]
                            cPE.inc(te.matmul(bank[:, col:col + 128], qsl, ksl,
                                              start=True, stop=True))
                    n_s2 = cPE.n

                    if len(la_q) >= 2:
                        la_q[0]("tail")
                        la_q.pop(0)

                    @block.vector
                    def _(v, n_a=n_a, use_b=use_b, n_s2=n_s2, ct=ct, pu=pu,
                          bankA=bankA, bankB=bankB):
                        cPE.wait(v, n_s2)
                        if ct > 0:
                            cDV.inc(v.tensor_tensor(bankA[:, 0:128],
                                                    bankA[:, 0:128],
                                                    CST["macb"][:, :], ALU.add))
                        if use_b:
                            cDV.inc(v.tensor_tensor(bankB[:, 0:128],
                                                    bankB[:, 0:128],
                                                    CST["mcb"][:, :], ALU.add))
                        else:
                            cDV.inc(v.tensor_tensor(
                                bankA[:, (n_a - 1) * 128:n_a * 128],
                                bankA[:, (n_a - 1) * 128:n_a * 128],
                                CST["mcb"][:, :], ALU.add))
                        cDV.wait(v)
                        if use_b:
                            cDV.inc(v.reduce_max(MN1[:, pu * 4:pu * 4 + 1],
                                                 bankA[:, 0:n_a * 128],
                                                 axis=AX.X, negate=True))
                            cDV.inc(v.reduce_max(MN1[:, pu * 4 + 1:pu * 4 + 2],
                                                 bankB[:, 0:128],
                                                 axis=AX.X, negate=True))
                            cDV.wait(v)
                            cDV.inc(v.tensor_tensor(MN1[:, pu * 4:pu * 4 + 1],
                                                    MN1[:, pu * 4:pu * 4 + 1],
                                                    MN1[:, pu * 4 + 1:pu * 4 + 2],
                                                    ALU.min))
                        else:
                            cDV.inc(v.reduce_max(MN1[:, pu * 4:pu * 4 + 1],
                                                 bankA[:, 0:n_a * 128],
                                                 axis=AX.X, negate=True))
                    n_mx = cDV.n
                    la["mx"][pu] = n_mx

                    @block.scalar
                    def _(s, n_a=n_a, use_b=use_b, n_mx=n_mx, pu=pu,
                          bankA=bankA, bankB=bankB):
                        cDV.wait(s, n_mx)
                        if use_b:
                            cAC.inc(s.activation(
                                PE1[:, pu * 512:pu * 512 + n_a * 128],
                                bankA[:, 0:n_a * 128],
                                AF.Exp, bias=MN1[:, pu * 4:pu * 4 + 1],
                                accum_out=SS2[:, pu * 4:pu * 4 + 1]))
                            cAC.inc(s.activation(
                                PE2[:, pu * 128:pu * 128 + 128],
                                bankB[:, 0:128],
                                AF.Exp, bias=MN1[:, pu * 4:pu * 4 + 1],
                                accum_out=SS2[:, pu * 4 + 1:pu * 4 + 2]))
                        else:
                            cAC.inc(s.activation(
                                PE1[:, pu * 512:pu * 512 + n_a * 128],
                                bankA[:, 0:n_a * 128], AF.Exp,
                                bias=MN1[:, pu * 4:pu * 4 + 1],
                                accum_out=SS2[:, pu * 4:pu * 4 + 1]))
                    n_e2 = cAC.n
                    la["exp"][pu] = n_e2

                    @block.vector
                    def _(v, use_b=use_b, n_e2=n_e2, pu=pu, n_a=n_a):
                        cAC.wait(v, n_e2)
                        if use_b:
                            cDV.inc(v.tensor_add(SS2[:, pu * 4:pu * 4 + 1],
                                                 SS2[:, pu * 4:pu * 4 + 1],
                                                 SS2[:, pu * 4 + 1:pu * 4 + 2]))
                            cDV.wait(v)
                        cDV.inc(v.reciprocal(SS2[:, pu * 4 + 3:pu * 4 + 4],
                                             SS2[:, pu * 4:pu * 4 + 1]))
                        cDV.wait(v)
                        if use_b:
                            cDV.inc(v.tensor_scalar(
                                PE1[:, pu * 512:pu * 512 + n_a * 128],
                                PE1[:, pu * 512:pu * 512 + n_a * 128],
                                SS2[:, pu * 4 + 3:pu * 4 + 4], None, ALU.mult))
                            cDV.inc(v.tensor_scalar(
                                PE2[:, pu * 128:pu * 128 + 128],
                                PE2[:, pu * 128:pu * 128 + 128],
                                SS2[:, pu * 4 + 3:pu * 4 + 4], None, ALU.mult))
                        else:
                            cDV.inc(v.tensor_scalar(
                                PE1[:, pu * 512:pu * 512 + n_a * 128],
                                PE1[:, pu * 512:pu * 512 + n_a * 128],
                                SS2[:, pu * 4 + 3:pu * 4 + 4], None, ALU.mult))
                    n_nm = cDV.n

                    def la_lag(part, u=u, pu=pu, nlive=nlive, n_nm=n_nm, h=h,
                               hh=hh, qc=qc, kbs=kbs, lo=lo, ct=ct, hp=hp):
                        if part == "head":
                            @block.tensor
                            def _(te, ptbfree=la["pts2"][(pu + 2) % 3]):
                                cDV.wait(te, n_nm)
                                cDV.wait(te, ptbfree)
                                for sl in range(nlive):
                                    src_ = PE1[:, pu * 512 + sl * 128:
                                               pu * 512 + sl * 128 + 128]                                         if sl < 4 else                                         PE2[:, pu * 128:pu * 128 + 128]
                                    cPE.inc(te.transpose(
                                        PTB[:, sl * 128:(sl + 1) * 128],
                                        src_, CST["ident"][:, :]))
                            npe_t = cPE.n

                            @block.vector
                            def _(v, npe_t=npe_t):
                                cPE.wait(v, npe_t)
                                cDV.inc(v.tensor_copy(
                                    PTS2[:, pu * 640:pu * 640 + nlive * 128],
                                    PTB[:, 0:nlive * 128]))
                            la["pts2"][pu] = cDV.n
                        else:
                            @block.tensor
                            def _(te, npts2=la["pts2"][pu], obffree=la["obf"]):
                                cDV.wait(te, npts2)
                                if u % 8 == 0:
                                    cDV.wait(te, obffree)
                                for kb in kbs:
                                    sl = kb - lo
                                    half_kb = ((ct - 1) % 2) if kb < 4                                         else (ct % 2)
                                    vslot = half_kb * 4 + (kb % 4)
                                    vsl = VTMR[:, vslot * 512 + h * 64:
                                               vslot * 512 + h * 64 + 64]
                                    cPE.inc(te.matmul(
                                        PO0[hh * 64:hh * 64 + 64,
                                            qc * 128:(qc + 1) * 128],
                                        vsl,
                                        PTS2[:, pu * 640 + sl * 128:
                                             pu * 640 + (sl + 1) * 128],
                                        start=(kb == kbs[0]),
                                        stop=(kb == kbs[-1])))
                            if u % 8 == 7:
                                n_pv = cPE.n

                                @block.vector
                                def _(v, n_pv=n_pv, hp=hp):
                                    cPE.wait(v, n_pv)
                                    cDV.inc(v.tensor_copy(
                                        OBF[:, hp * 512:(hp + 1) * 512],
                                        PO0[:, :]))
                                la["obf"] = cDV.n
                    la_q.append(la_lag)

            while la_q:
                la_q[0]("head")
                la_q[0]("tail")
                la_q.pop(0)

            wo_dv2 = [0, 0]
            for ofc in range(NPC):
                pbank = PB0 if ofc % 2 == 0 else PB1

                @block.tensor
                def _(te, ofc=ofc, pbank=pbank, bankfree=wo_dv2[ofc % 2]):
                    if ofc < 2:
                        cAC.wait(te)
                        cDV.wait(te)
                    else:
                        cDV.wait(te, bankfree)
                    for kc in range(4):
                        cPE.inc(te.matmul(
                            pbank[:, :],
                            FWO[:, kc * 1024 + ofc * 128:kc * 1024 + ofc * 128 + 128],
                            OBF[:, kc * 512:(kc + 1) * 512],
                            start=(kc == 0), stop=(kc == 3)))

                @block.vector
                def _(v, ofc=ofc, pbank=pbank, t0=t0):
                    cPE.wait(v)
                    cDV.inc(v.tensor_add(XF[ofc][:, 1 + t0:1 + t0 + CT],
                                         pbank[:, :],
                                         XF[ofc][:, 1 + t0:1 + t0 + CT]))
                wo_dv2[ofc % 2] = cDV.n
            n_woupd = cDV.n

            @block.gpsimd
            def _(g, t0=t0, n_woupd=n_woupd):
                cDV.wait(g, n_woupd)
                cPE.wait(g)
                for pc in range(NPC):
                    cPL.inc(g.tensor_copy(XFB[:, pc * 513:(pc + 1) * 513],
                                          XF[pc][:, t0:t0 + 513]))
            n_xfb2 = cPL.n

            @block.tensor
            def _(te, n_xfb2=n_xfb2):
                cPL.wait(te, n_xfb2)
                for pc in range(NPC):
                    off = 1 if pc < 4 else 0
                    cPE.inc(te.matmul(PB0[:, :], CST["onesl"][:, :],
                                      XFB[:, pc * 513 + off:pc * 513 + off + 512],
                                      start=(pc == 0), stop=(pc == NPC - 1)))
            n_mu2 = cPE.n

            msq2_pe = [0, 0]
            xsq2_dv = [0, 0]
            for pc in range(NPC):
                par = pc % 2
                off = 1 if pc < 4 else 0

                @block.vector
                def _(v, pc=pc, par=par, t0=t0, sqfree=msq2_pe[par]):
                    if pc >= 2:
                        cPE.wait(v, sqfree)
                    cDV.inc(v.tensor_mul(XSQ2[:, par * 513:par * 513 + 513],
                                         XF[pc][:, t0:t0 + 513],
                                         XF[pc][:, t0:t0 + 513]))
                xsq2_dv[par] = cDV.n

                @block.tensor
                def _(te, pc=pc, par=par, off=off, nsq=xsq2_dv[par]):
                    cDV.wait(te, nsq)
                    cPE.inc(te.matmul(PB1[:, :], CST["onesl"][:, :],
                                      XSQ2[:, par * 513 + off:par * 513 + off + 512],
                                      start=(pc == 0), stop=(pc == NPC - 1)))
                msq2_pe[par] = cPE.n
            n_msq2 = cPE.n

            @block.scalar
            def _(s, n_mu2=n_mu2):
                cPE.wait(s, n_mu2)
                cAC.inc(s.activation(MU2[:, :], PB0[:, :], AF.Square))

            @block.vector
            def _(v, t0=t0, n_msq2=n_msq2):
                cAC.wait(v)
                cPE.wait(v, n_msq2)
                cDV.inc(v.tensor_sub(VART[:, :], PB1[:, :], MU2[:, :]))
                cDV.inc(v.tensor_copy(MUS[:, t0:t0 + CT], PB0[:, :]))

            @block.scalar
            def _(s, t0=t0):
                cDV.wait(s)
                cAC.inc(s.activation(VART[:, :], VART[:, :], AF.Sqrt,
                                     bias=pv("eps5")))
            n_mvs = cAC.n

            @block.vector
            def _(v, t0=t0, n_mvs=n_mvs):
                cAC.wait(v, n_mvs)
                cDV.inc(v.reciprocal(RS[:, t0:t0 + CT], VART[:, :]))

        ph.close()

    def emit_ffn_sweep(seq, l, MUS, RS):
        ph = contextlib.ExitStack()
        W1A = mksb(ph, "w1a", [128, 2 * 4096], BF16)
        W1G = mksb(ph, "w1g", [128, 2 * 4096], BF16)
        W2S = mksb(ph, "w2s", [128, 2 * 4096], BF16)
        HNB = mksb(ph, "hnb", [128, 4096], BF16)
        HM = mksb(ph, "hm", [128, NFC * 512], BF16)
        G32 = mksb(ph, "g32", [128, 1024], F32)
        OUTP = mksb(ph, "outp", [128, 4096], F32)
        T4 = mksb(ph, "t4", [128, 512], F32)

        first_sync = [True]
        g32_n = [0, 0]
        hm_n = [0, 0]
        w1_te_n = [0, 0]
        w2_te_n = [0, 0]

        def sguard(sy):
            if first_sync[0]:
                phase_guard(sy)
                first_sync[0] = False

        for ct in reversed(range(NCT if F_SEC != 0 else 0)):
            t0 = ct * CT
            n_outp = None
            if l == 1:
                @block.sync
                def _(sy, seq=seq, t0=t0):
                    sguard(sy)
                    cPL.wait(sy)
                    cSTp[0].wait(sy)
                    cSTp[1].wait(sy)
                    src = AP(out_h.tensor, seq * DM * NTOK + t0,
                             [[NTOK, 128], [128 * NTOK, NPC], [1, CT]])
                    cOP.inc(sy.dma_start(out=OUTP[:, :], in_=src), 16)
                n_outp = cOP.n

            for pc in range(NPC):
                off = 1 if pc < 4 else 0
                par = pc % 2

                @block.vector
                def _(v, pc=pc, par=par, t0=t0, off=off):
                    cPL.wait(v)
                    cDV.wait(v)
                    if pc == 0:
                        cAC.wait(v)
                    cDV.inc(v.tensor_sub(TT[:, par * 512:(par + 1) * 512],
                                         XF[pc][:, t0 + off:t0 + off + CT],
                                         MUS[:, t0:t0 + CT]))

                @block.vector
                def _(v2, par=par, t0=t0):
                    cDV.wait(v2)
                    cPL.inc(v2.tensor_tensor(TT[:, par * 512:(par + 1) * 512],
                                             TT[:, par * 512:(par + 1) * 512],
                                             RS[:, t0:t0 + CT], ALU.mult))

                @block.vector
                def _(v, pc=pc, par=par, l=l):
                    cPL.wait(v)
                    cDV.inc(v.tensor_scalar(HNB[:, pc * 512:(pc + 1) * 512],
                                            TT[:, par * 512:(par + 1) * 512],
                                            pv(("lng", l, pc)),
                                            pv(("lnb", l, pc)),
                                            ALU.mult, ALU.add))
            n_hnb = cAC.n
            if F_SEC is not None and F_SEC <= 1:
                continue

            fc0 = 0
            for gi, gsz in enumerate(FGROUPS):
                par = gi % 2

                @block.sync
                def _(sy, l=l, fc0=fc0, gsz=gsz, par=par, w1free=w1_te_n[par]):
                    sguard(sy)
                    cPE.wait(sy, w1free)
                    src = AP(wb_h["fw1a"].tensor, l * DM * HIDP + fc0 * 128,
                             [[HIDP, 128], [128 * HIDP, NPC], [1, gsz * 128]])
                    cW1p[par].inc(sy.dma_start(
                        out=W1A[:, par * 4096:par * 4096 + NPC * gsz * 128]
                        .rearrange("p (a b) -> p a b", a=NPC),
                        in_=src), 16)

                @block.gpsimd
                def _(g, l=l, fc0=fc0, gsz=gsz, par=par, w1free=w1_te_n[par],
                      gi=gi, ct=ct):
                    if gi == 0 and ct == NCT - 1:
                        cPE.wait(g)
                        cDV.wait(g)
                        cAC.wait(g)
                    cPE.wait(g, w1free)
                    src = AP(wb_h["fw1g"].tensor, l * DM * HIDP + fc0 * 128,
                             [[HIDP, 128], [128 * HIDP, NPC], [1, gsz * 128]])
                    cW1g[par].inc(g.dma_start(
                        out=W1G[:, par * 4096:par * 4096 + NPC * gsz * 128]
                        .rearrange("p (a b) -> p a b", a=NPC),
                        in_=src), 16)
                n_w1 = cW1p[par].n
                n_w1g = cW1g[par].n

                for fcl in range(gsz):
                    fc = fc0 + fcl
                    parf = fc % 2
                    pa = [PB0, PB1][parf]
                    pg = [PS0, PS1][parf]

                    @block.tensor
                    def _(te, fcl=fcl, gsz=gsz, par=par, pa=pa, pg=pg, n_w1=n_w1,
                          fc=fc, n_hnb=n_hnb, pafree=hm_n[parf],
                          pgfree=g32_n[parf], n_w1g=n_w1g):
                        te.wait_ge(cW1p[par].sem, n_w1)
                        te.wait_ge(cW1g[par].sem, n_w1g)
                        if fc == 0:
                            cAC.wait(te, n_hnb)
                            cDV.wait(te)
                        cAC.wait(te, pgfree)
                        cDV.wait(te, pafree)
                        for kc in range(NPC):
                            cPE.inc(te.matmul(
                                pa[:, :],
                                W1A[:, par * 4096 + kc * gsz * 128 + fcl * 128:
                                    par * 4096 + kc * gsz * 128 + fcl * 128 + 128],
                                HNB[:, kc * 512:(kc + 1) * 512],
                                start=(kc == 0), stop=(kc == NPC - 1)))
                        for kc in range(NPC):
                            cPE.inc(te.matmul(
                                pg[:, :],
                                W1G[:, par * 4096 + kc * gsz * 128 + fcl * 128:
                                    par * 4096 + kc * gsz * 128 + fcl * 128 + 128],
                                HNB[:, kc * 512:(kc + 1) * 512],
                                start=(kc == 0), stop=(kc == NPC - 1)))
                    n_ag = cPE.n
                    w1_te_n[par] = n_ag

                    @block.scalar
                    def _(s, fc=fc, parf=parf, pg=pg, l=l, n_ag=n_ag,
                          g32free=hm_n[parf]):
                        cPE.wait(s, n_ag)
                        cDV.wait(s, g32free)
                        cAC.inc(s.activation(G32[:, parf * 512:(parf + 1) * 512],
                                             pg[:, :], AF.Gelu,
                                             bias=pv(("b1g", l, fc))))
                    g32_n[parf] = cAC.n

                    @block.vector
                    def _(v, fc=fc, parf=parf, pa=pa, l=l, n_ag=n_ag,
                          ng32=g32_n[parf]):
                        cPE.wait(v, n_ag)
                        cAC.wait(v, ng32)
                        cDV.inc(v.scalar_tensor_tensor(
                            HM[:, fc * 512:(fc + 1) * 512], pa[:, :],
                            pv(("b1a", l, fc)),
                            G32[:, parf * 512:(parf + 1) * 512],
                            ALU.add, ALU.mult))
                    hm_n[parf] = cDV.n
                fc0 += gsz
            n_hm = cDV.n
            if F_SEC is not None and F_SEC <= 2:
                continue

            hc0 = 0
            for gi, gsz in enumerate(FGROUPS):
                par = gi % 2

                @block.scalar
                def _(v, l=l, hc0=hc0, gsz=gsz, par=par, w2free=w2_te_n[par]):
                    cPE.wait(v, w2free)
                    src = AP(wb_h["fw2"].tensor, l * HIDP * DM + hc0 * 128 * DM,
                             [[DM, 128], [128 * DM, gsz], [1, DM]])
                    cW2v[par].inc(v.dma_start(
                        out=W2S[:, par * 4096:par * 4096 + gsz * 1024]
                        .rearrange("p (a b) -> p a b", a=gsz),
                        in_=src), 16)
                n_w2 = cW2v[par].n

                for hcl in range(gsz):
                    hc = hc0 + hcl

                    @block.tensor
                    def _(te, hc=hc, hcl=hcl, par=par, n_w2=n_w2, n_hm=n_hm):
                        te.wait_ge(cW2v[par].sem, n_w2)
                        if hc == 0:
                            cDV.wait(te, n_hm)
                            cDV.wait(te)
                        for ofc in range(7):
                            cPE.inc(te.matmul(
                                PW[ofc][:, :],
                                W2S[:, par * 4096 + hcl * 1024 + ofc * 128:
                                    par * 4096 + hcl * 1024 + ofc * 128 + 128],
                                HM[:, hc * 512:(hc + 1) * 512],
                                start=(hc == 0), stop=(hc == NFC - 1)))
                w2_te_n[par] = cPE.n
                hc0 += gsz
            n_w2m = cPE.n

            for ofc in range(7):
                @block.vector
                def _(v, ofc=ofc, l=l, t0=t0, n_w2m=n_w2m):
                    cPE.wait(v, n_w2m)
                    if l == 1 and ofc > 0:
                        cPL.wait(v)
                    if l == 0:
                        cDV.inc(v.scalar_tensor_tensor(
                            XF[ofc][:, 1 + t0:1 + t0 + CT], PW[ofc][:, :],
                            pv(("b2", 0, ofc)), XF[ofc][:, 1 + t0:1 + t0 + CT],
                            ALU.add, ALU.add))
                    else:
                        cDV.inc(v.scalar_tensor_tensor(
                            T4[:, :], PW[ofc][:, :], pv(("b2", 1, ofc)),
                            XF[ofc][:, 1 + t0:1 + t0 + CT], ALU.add, ALU.add))

                if l == 1:
                    @block.gpsimd
                    def _(g, ofc=ofc, n_outp=n_outp):
                        cDV.wait(g)
                        g.wait_ge(cOP.sem, n_outp)
                        cPL.inc(g.tensor_tensor(
                            OUTP[:, ofc * 512:(ofc + 1) * 512], T4[:, :],
                            OUTP[:, ofc * 512:(ofc + 1) * 512], ALU.add))

            if F_SEC is not None and F_SEC <= 3:
                continue
            hc0 = 0
            for gi, gsz in enumerate(FGROUPS):
                par = gi % 2

                @block.gpsimd
                def _(g, l=l, hc0=hc0, gsz=gsz, par=par):
                    cPE.wait(g)
                    src = AP(wb_h["fw2"].tensor,
                             l * HIDP * DM + hc0 * 128 * DM + 7 * 128,
                             [[DM, 128], [128 * DM, gsz], [1, 128]])
                    cW2g[par].inc(g.dma_start(
                        out=W2S[:, par * 4096:par * 4096 + gsz * 128]
                        .rearrange("p (a b) -> p a b", a=gsz),
                        in_=src), 16)
                n_w27 = cW2g[par].n

                @block.tensor
                def _(te, hc0=hc0, gsz=gsz, par=par, n_w27=n_w27):
                    te.wait_ge(cW2g[par].sem, n_w27)
                    if hc0 == 0:
                        cDV.wait(te)
                    for hcl in range(gsz):
                        hc = hc0 + hcl
                        cPE.inc(te.matmul(
                            PW[0][:, :],
                            W2S[:, par * 4096 + hcl * 128:par * 4096 + hcl * 128 + 128],
                            HM[:, hc * 512:(hc + 1) * 512],
                            start=(hc == 0), stop=(hc == NFC - 1)))
                w2_te_n[par] = cPE.n
                hc0 += gsz
            n_w27m = cPE.n

            @block.vector
            def _(v, l=l, t0=t0, n_w27m=n_w27m):
                cPE.wait(v, n_w27m)
                if l == 1:
                    cPL.wait(v)
                if l == 0:
                    cDV.inc(v.scalar_tensor_tensor(
                        XF[7][:, 1 + t0:1 + t0 + CT], PW[0][:, :],
                        pv(("b2", 0, 7)), XF[7][:, 1 + t0:1 + t0 + CT],
                        ALU.add, ALU.add))
                else:
                    cDV.inc(v.scalar_tensor_tensor(
                        T4[:, :], PW[0][:, :], pv(("b2", 1, 7)),
                        XF[7][:, 1 + t0:1 + t0 + CT], ALU.add, ALU.add))

            if l == 1:
                @block.gpsimd
                def _(g, n_outp=n_outp):
                    cDV.wait(g)
                    g.wait_ge(cOP.sem, n_outp)
                    cPL.inc(g.tensor_tensor(OUTP[:, 7 * 512:8 * 512], T4[:, :],
                                            OUTP[:, 7 * 512:8 * 512], ALU.add))
                n_fin = cPL.n

                @block.sync
                def _(sy, seq=seq, t0=t0, n_fin=n_fin, ct=ct):
                    cPL.wait(sy, n_fin)
                    dst = AP(out_h.tensor, seq * DM * NTOK + t0,
                             [[NTOK, 128], [128 * NTOK, NPC], [1, CT]])
                    cSTp[ct % 2].inc(sy.dma_start(out=dst, in_=OUTP[:, :]), 16)

        ph.close()

    phases = ["A", "attn0", "ffn0", "attn1", "ffn1"]
    nseq = 1 if PH_LIMIT is not None else BUILD_SEQS
    plim = PH_LIMIT if PH_LIMIT is not None else 5
    for seq in range(nseq):
        todo = phases[:plim]
        if "A" in todo:
            emit_phase_a(seq)
        for l in (0, 1):
            if f"attn{l}" not in todo and f"ffn{l}" not in todo:
                continue
            lay = contextlib.ExitStack()
            MUS = mksb(lay, "mus", [128, NTOK], F32)
            RS = mksb(lay, "rs", [128, NTOK], F32)
            if f"attn{l}" in todo:
                emit_attn_sweep(seq, l, MUS, RS)
            if f"ffn{l}" in todo:
                emit_ffn_sweep(seq, l, MUS, RS)
            lay.close()

    if PH_LIMIT is not None:
        @block.sync
        def _(sy):
            cDV.wait(sy)
            cAC.wait(sy)
            cPL.wait(sy)
            cPE.wait(sy)
            for pc in range(NPC):
                cSTp[0].inc(sy.dma_start(out=dbg_h[pc * 128:(pc + 1) * 128, :],
                                         in_=XF[pc][:, 1:1 + NTOK]), 16)

    @block.sync
    def _(sy):
        sy.wait_ge(cSTp[0].sem, cSTp[0].n)
        sy.wait_ge(cSTp[1].sem, cSTp[1].n)

    stack.close()
    return nc


def kernel(**inputs):
    x = np.asarray(inputs["L0_attn_x"], dtype=np.float32)
    xt = np.ascontiguousarray(x.transpose(0, 2, 1))
    consts = _host_consts()
    pvec, cmap = _host_pvec(inputs)
    w = _host_weights(inputs)

    nc = build_nc(cmap, pvec.shape[1])

    in_maps = []
    for c in range(NCORES):
        m = {"x": xt[c * NSEQ:(c + 1) * NSEQ]}
        m.update(w)
        m.update(consts)
        m["pvec"] = pvec
        in_maps.append(m)
    res = run_bass_kernel_spmd(nc, in_maps, core_ids=list(range(NCORES)))
    out = np.concatenate([r["out"] for r in res.results], axis=0)
    return np.ascontiguousarray(out.transpose(0, 2, 1))


# version 13 (final)
# speedup vs baseline: 1.0139x; 1.0139x over previous
import numpy as np
import ml_dtypes
import contextlib
import concourse.bass as bass
import concourse.mybir as mybir
from concourse.bass_utils import run_bass_kernel_spmd
from concourse.ap import AP

F32 = mybir.dt.float32
BF16 = mybir.dt.bfloat16
AF = mybir.ActivationFunctionType
ALU = mybir.AluOpType
AX = mybir.AxisListType

NCORES = 8
NSEQ = 4
NTOK = 2048
CT = 512
NCT = NTOK // CT
DM = 1024
NPC = DM // 128
HIDP = 2816
NFC = HIDP // 128
FGROUPS = [4, 4, 4, 4, 4, 2]

PH_LIMIT = None
BUILD_SEQS = NSEQ
A_SEC = None
A_CT = None
A_PAIR = None
F_SEC = None


def _host_consts():
    c = {}
    d = np.arange(64)
    half = d % 32
    inv_freq = 1.0 / (10000.0 ** (2.0 * half / 64.0))
    t = np.arange(NTOK, dtype=np.float64)
    freqs = t[None, :] * inv_freq[:, None]
    base = (2.0 * half + 25.6) / 89.6
    power = (t - NTOK // 2) / 256.0
    scale = base[:, None] ** power[None, :]
    sign = np.where(d < 32, -1.0, 1.0)[:, None]
    c["rotqc"] = np.tile((8.0 * np.cos(freqs) * scale), (2, 1)).astype(np.float32)
    c["rotqs"] = np.tile((sign * 8.0 * np.sin(freqs) * scale), (2, 1)).astype(np.float32)
    c["rotkc"] = np.tile((np.cos(freqs) / scale), (2, 1)).astype(np.float32)
    c["rotks"] = np.tile((sign * np.sin(freqs) / scale), (2, 1)).astype(np.float32)

    c["ident"] = np.eye(128, dtype=ml_dtypes.bfloat16)

    s = np.arange(128)[:, None]
    j = np.arange(128)[None, :]
    allowed = ((s // 64) == (j // 64)) & ((j % 64) <= (s % 64))
    amask = np.where(allowed, 0.0, -1e9).astype(np.float32)
    c["amsk4"] = np.tile(amask, (1, 4)).astype(ml_dtypes.bfloat16)
    mc = np.where(j <= s, 0.0, -1e9).astype(np.float32)
    mac = np.where(j >= s, 0.0, -1e9).astype(np.float32)
    c["mcb"] = mc.astype(ml_dtypes.bfloat16)
    c["macb"] = mac.astype(ml_dtypes.bfloat16)
    c["onesl"] = np.full((128, 128), 1.0 / 1024.0, dtype=ml_dtypes.bfloat16)
    bd = np.zeros((128, 128), dtype=np.float32)
    bd[:64, :64] = 1.0
    bd[64:, 64:] = 1.0
    c["bd64"] = bd.astype(ml_dtypes.bfloat16)
    cum = np.zeros((128, 36), dtype=np.float32)
    for tc in range(4):
        for cc in range(9):
            tglob = tc * 128 + np.arange(128)
            lim = 64 * cc if cc < 8 else CT
            cum[:, tc * 9 + cc] = np.where(tglob < lim, 1.0 / 64.0, 0.0)
    c["cumloc"] = cum.astype(ml_dtypes.bfloat16)
    sel = np.zeros((128, 4 * 64), dtype=np.float32)
    srow = np.arange(128)
    for p in range(4):
        for slot in range(8):
            for cc in range(8):
                sel[:, p * 64 + slot * 8 + cc] = np.where(cc == 2 * p + srow // 64,
                                                          0.5, 0.0)
    c["selt8"] = sel
    return c


def _host_pvec(inputs):
    cols, cmap = [], {}

    def add(key, vec):
        cmap[key] = len(cols)
        cols.append(np.asarray(vec, dtype=np.float32))

    qs = np.asarray(inputs["L0_ffn_q_scale"])
    ks = np.asarray(inputs["L0_ffn_k_scale"])
    lng = np.asarray(inputs["L0_ffn_ln_g"])
    lnb = np.asarray(inputs["L0_ffn_ln_b"])
    b1 = np.asarray(inputs["L0_ffn_b1"])
    b2 = np.asarray(inputs["L0_ffn_b2"])
    b1a = np.zeros((2, HIDP), np.float32)
    b1g = np.zeros((2, HIDP), np.float32)
    b1a[:, :2730] = b1[:, :2730]
    b1g[:, :2730] = b1[:, 2730:]
    for l in range(2):
        qt = np.tile(qs[l], 8)
        kt = np.tile(ks[l], 8)
        for pc in range(4):
            add(("qs", l, pc), qt[pc * 128:(pc + 1) * 128])
            add(("ks", l, pc), kt[pc * 128:(pc + 1) * 128])
        for pc in range(NPC):
            add(("lng", l, pc), lng[l, pc * 128:(pc + 1) * 128])
            add(("lnb", l, pc), lnb[l, pc * 128:(pc + 1) * 128])
            add(("b2", l, pc), b2[l, pc * 128:(pc + 1) * 128])
        for fc in range(NFC):
            add(("b1a", l, fc), b1a[l, fc * 128:(fc + 1) * 128])
            add(("b1g", l, fc), b1g[l, fc * 128:(fc + 1) * 128])
    add("eps5", np.full(128, 1e-5, np.float32))
    add("eps24", np.full(128, 1e-24, np.float32))
    return np.stack(cols, axis=1), cmap


def _host_weights(inputs):
    w = {}
    w["awqkv"] = np.ascontiguousarray(inputs["L0_attn_W_qkv"], dtype=np.float32)
    w["awout"] = np.ascontiguousarray(inputs["L0_attn_W_out"], dtype=np.float32)
    w["fwqkv"] = np.ascontiguousarray(inputs["L0_ffn_wqkv"], dtype=np.float32)
    w["fwo"] = np.ascontiguousarray(inputs["L0_ffn_wo"], dtype=np.float32)
    w1 = np.asarray(inputs["L0_ffn_w1"], dtype=np.float32)
    w1a = np.zeros((2, DM, HIDP), np.float32)
    w1g = np.zeros((2, DM, HIDP), np.float32)
    w1a[:, :, :2730] = w1[:, :, :2730]
    w1g[:, :, :2730] = w1[:, :, 2730:]
    w["fw1a"] = w1a
    w["fw1g"] = w1g
    w2 = np.asarray(inputs["L0_ffn_w2"], dtype=np.float32)
    w2p = np.zeros((2, HIDP, DM), np.float32)
    w2p[:, :2730, :] = w2
    w["fw2"] = w2p
    return w


class Ctr:
    def __init__(self, stack, nc, name):
        self.sem = stack.enter_context(nc.semaphore(name))
        self.n = 0

    def inc(self, instr, k=1):
        instr.then_inc(self.sem, k)
        self.n += k
        return self.n

    def wait(self, eng, n=None):
        eng.wait_ge(self.sem, self.n if n is None else n)


def build_nc(cmap_pv, npv):
    nc = bass.Bass("TRN2", target_bir_lowering=False)
    stack = contextlib.ExitStack()

    x_h = nc.dram_tensor("x", [NSEQ, DM, NTOK], F32, kind="ExternalInput").ap()
    out_h = nc.dram_tensor("out", [NSEQ, DM, NTOK], F32, kind="ExternalOutput").ap()
    w_h = {
        "awqkv": nc.dram_tensor("awqkv", [DM, 3072], F32, kind="ExternalInput").ap(),
        "awout": nc.dram_tensor("awout", [DM, DM], F32, kind="ExternalInput").ap(),
        "fwqkv": nc.dram_tensor("fwqkv", [2, DM, 1536], F32, kind="ExternalInput").ap(),
        "fwo": nc.dram_tensor("fwo", [2, 512, DM], F32, kind="ExternalInput").ap(),
        "fw1a": nc.dram_tensor("fw1a", [2, DM, HIDP], F32, kind="ExternalInput").ap(),
        "fw1g": nc.dram_tensor("fw1g", [2, DM, HIDP], F32, kind="ExternalInput").ap(),
        "fw2": nc.dram_tensor("fw2", [2, HIDP, DM], F32, kind="ExternalInput").ap(),
    }
    wb_h = {
        "awqkv": nc.dram_tensor("awqkv_b", [DM, 3072], BF16, kind="Internal").ap(),
        "awout": nc.dram_tensor("awout_b", [DM, DM], BF16, kind="Internal").ap(),
        "fwqkv": nc.dram_tensor("fwqkv_b", [2, DM, 1536], BF16, kind="Internal").ap(),
        "fwo": nc.dram_tensor("fwo_b", [2, 512, DM], BF16, kind="Internal").ap(),
        "fw1a": nc.dram_tensor("fw1a_b", [2, DM, HIDP], BF16, kind="Internal").ap(),
        "fw1g": nc.dram_tensor("fw1g_b", [2, DM, HIDP], BF16, kind="Internal").ap(),
        "fw2": nc.dram_tensor("fw2_b", [2, HIDP, DM], BF16, kind="Internal").ap(),
    }
    cst_h = {}
    for name, shape, dt in [
        ("rotqc", [128, NTOK], F32), ("rotqs", [128, NTOK], F32),
        ("rotkc", [128, NTOK], F32), ("rotks", [128, NTOK], F32),
        ("ident", [128, 128], BF16), ("amsk4", [128, 512], BF16),
        ("mcb", [128, 128], BF16),
        ("macb", [128, 128], BF16), ("onesl", [128, 128], BF16),
        ("bd64", [128, 128], BF16), ("cumloc", [128, 36], BF16),
        ("selt8", [128, 256], F32), ("pvec", [128, npv], F32),
    ]:
        cst_h[name] = nc.dram_tensor(name, shape, dt, kind="ExternalInput").ap()

    dbg_h = None
    if PH_LIMIT is not None:
        dbg_h = nc.dram_tensor("dbg", [DM, NTOK], F32, kind="ExternalOutput").ap()

    _names = [0]

    def mksb(st, name, shape, dt):
        _names[0] += 1
        return st.enter_context(nc.sbuf_tensor(f"{name}_{_names[0]}", shape, dt))

    psa = lambda name, shape, dt: stack.enter_context(nc.psum_tensor(name, shape, dt))

    XF = [mksb(stack, f"xf{pc}", [128, NTOK + 1], F32) for pc in range(NPC)]
    CST = {}
    for k in ("ident", "amsk4", "mcb", "macb", "onesl", "bd64", "cumloc"):
        CST[k] = mksb(stack, k, list(cst_h[k].tensor.shape), BF16)
    CST["selt8"] = mksb(stack, "selt8", [128, 256], F32)
    PVEC = mksb(stack, "pvec", [128, npv], F32)
    pv = lambda key: PVEC[:, cmap_pv[key]:cmap_pv[key] + 1]
    CARRYK = mksb(stack, "carryk", [128, 8], F32)
    CARRYV = mksb(stack, "carryv", [128, 8], F32)
    TT = mksb(stack, "tt", [128, 1024], F32)

    PB0 = psa("pb0", [128, 512], F32)
    PB1 = psa("pb1", [128, 512], F32)
    PS0 = psa("ps0", [128, 512], F32)
    PS1 = psa("ps1", [128, 512], F32)
    PO0 = psa("po0", [128, 512], F32)
    PO1 = psa("po1", [128, 512], F32)
    PGX = psa("pgx", [128, 512], F32)
    PTB = psa("ptb", [128, 1024], BF16)
    PW = [PB0, PB1, PS0, PS1, PO0, PO1, PGX]

    block = stack.enter_context(nc.Block())

    cLX = Ctr(stack, nc, "c_lx")
    cLW = Ctr(stack, nc, "c_lw")
    cLQ = Ctr(stack, nc, "c_lq")
    cW1p = [Ctr(stack, nc, "c_w1a"), Ctr(stack, nc, "c_w1b")]
    cW1g = [Ctr(stack, nc, "c_w1ga"), Ctr(stack, nc, "c_w1gb")]
    cW2p = [Ctr(stack, nc, "c_w2a"), Ctr(stack, nc, "c_w2b")]
    cW2g2 = Ctr(stack, nc, "c_w2g2")
    cW2g = [Ctr(stack, nc, "c_w2ga"), Ctr(stack, nc, "c_w2gb")]
    cOP = Ctr(stack, nc, "c_op")
    cCVp = [Ctr(stack, nc, "c_cv0"), Ctr(stack, nc, "c_cv1")]
    cSTc = [Ctr(stack, nc, "c_sc0"), Ctr(stack, nc, "c_sc1")]
    cSTp = [Ctr(stack, nc, "c_st0"), Ctr(stack, nc, "c_st1")]
    cPE = Ctr(stack, nc, "c_pe")
    cDV = Ctr(stack, nc, "c_dv")
    cAC = Ctr(stack, nc, "c_ac")
    cPL = Ctr(stack, nc, "c_pl")

    def phase_guard(sy):
        cPE.wait(sy)
        cDV.wait(sy)
        cAC.wait(sy)
        cPL.wait(sy)
        cSTp[0].wait(sy)
        cSTp[1].wait(sy)

    def conv_wait(sy):
        sy.wait_ge(cSTc[0].sem, conv_all_stored[0])
        sy.wait_ge(cSTc[1].sem, conv_all_stored[1])

    conv_chunks = []

    def add_conv(src, dst, nrows, ncols):
        for r0 in range(0, nrows, 128):
            conv_chunks.append((src, dst, r0, ncols))

    add_conv(w_h["awqkv"], wb_h["awqkv"], DM, 3072)
    add_conv(w_h["awout"], wb_h["awout"], DM, DM)
    for l in range(2):
        add_conv(w_h["fwqkv"][l], wb_h["fwqkv"][l], DM, 1536)
        add_conv(w_h["fwo"][l], wb_h["fwo"][l], 512, DM)
        add_conv(w_h["fw1a"][l], wb_h["fw1a"][l], DM, HIDP)
        add_conv(w_h["fw1g"][l], wb_h["fw1g"][l], DM, HIDP)
        add_conv(w_h["fw2"][l], wb_h["fw2"][l], HIDP, DM)

    cvst = contextlib.ExitStack()
    STG = mksb(cvst, "stg", [128, 2 * 5632], BF16)
    conv_store_done = [0] * len(conv_chunks)
    for i, (src, dst, r0, ncols) in enumerate(conv_chunks):
        par = i % 2

        @block.gpsimd
        def _(gp, i=i, src=src, r0=r0, ncols=ncols, par=par):
            if i >= 2:
                cSTc[par].wait(gp, conv_store_done[i - 2])
            cCVp[par].inc(gp.dma_start(out=STG[:, par * 5632:par * 5632 + ncols],
                                       in_=src[r0:r0 + 128, :]), 16)
        n_cv = cCVp[par].n

        @block.sync
        def _(sy, i=i, dst=dst, r0=r0, ncols=ncols, par=par, n_cv=n_cv):
            sy.wait_ge(cCVp[par].sem, n_cv)
            conv_store_done[i] = cSTc[par].inc(
                sy.dma_start(out=dst[r0:r0 + 128, :],
                             in_=STG[:, par * 5632:par * 5632 + ncols]), 16)
    conv_all_stored = (cSTc[0].n, cSTc[1].n)
    cvst.close()

    @block.vector
    def _(v):
        for pc in range(NPC):
            cDV.inc(v.memset(XF[pc][:, 0:1], 0.0))

    @block.sync
    def _(sy):
        for k in ("ident", "amsk4", "mcb", "macb", "onesl", "bd64",
                  "cumloc", "selt8"):
            cLW.inc(sy.dma_start(out=CST[k][:, :], in_=cst_h[k][:, :]), 16)
        cLW.inc(sy.dma_start(out=PVEC[:, :], in_=cst_h["pvec"][:, :]), 16)
    n_consts = cLW.n

    def emit_phase_a(seq):
        ph = contextlib.ExitStack()
        X32 = mksb(ph, "x32", [128, 4096], F32)
        XB = mksb(ph, "xb", [128, 1024], BF16)
        XSQ = mksb(ph, "xsq", [128, 1024], BF16)
        XNB = mksb(ph, "xnb", [128, 4096], BF16)
        QB = mksb(ph, "qb", [128, 4096], BF16)
        KB = mksb(ph, "kb", [128, 4096], BF16)
        VTM = mksb(ph, "vtm", [128, 4096], BF16)
        KTMt = mksb(ph, "ktmt", [128, 512], BF16)
        VT = mksb(ph, "vt", [128, 512], BF16)
        CK9 = mksb(ph, "ck9", [128, 18], F32)
        CKB = mksb(ph, "ckb", [128, 64], BF16)
        CVB = mksb(ph, "cvb", [128, 64], BF16)
        CVT = mksb(ph, "cvt", [8, 1024], BF16)
        OB = mksb(ph, "ob", [128, 4096], BF16)
        PEXP = mksb(ph, "pexp", [128, 1024], BF16)
        PTS = mksb(ph, "pts", [128, 1024], BF16)
        TST = mksb(ph, "tst", [8, 1024], BF16)
        SSUM = mksb(ph, "ssum", [128, 8], F32)
        SREC = mksb(ph, "srec", [128, 8], F32)
        EG = mksb(ph, "eg", [128, 64], F32)
        SG = mksb(ph, "sg", [128, 64], F32)
        TS_ = mksb(ph, "ts", [128, 64], BF16)
        MU2 = mksb(ph, "mu2", [128, 512], F32)
        VART = mksb(ph, "vart", [128, 512], F32)
        RREP = mksb(ph, "rrep", [128, 512], F32)
        AWQKV = mksb(ph, "awqkv", [128, 8 * 1536], BF16)
        AWOUT = mksb(ph, "awout", [128, 8 * 1024], BF16)

        @block.sync
        def _(sy):
            phase_guard(sy)
            conv_wait(sy)
            src = AP(wb_h["awout"].tensor, 0,
                     [[1024, 128], [128 * 1024, NPC], [1, 1024]])
            cLW.inc(sy.dma_start(
                out=AWOUT[:, :].rearrange("p (a b) -> p a b", a=NPC),
                in_=src), 16)
        n_aw = cLW.n

        @block.vector
        def _(v):
            v.memset(CARRYK[:, :], 0.0)
            cDV.inc(v.memset(CARRYV[:, :], 0.0))

        for ct in range(NCT if A_CT is None else A_CT):
            t0 = ct * CT

            @block.sync
            def _(sy, seq=seq, t0=t0, ct=ct):
                cDV.wait(sy)
                cSTp[(ct - 1) % 2].wait(sy)
                src = AP(x_h.tensor, seq * DM * NTOK + t0,
                         [[NTOK, 128], [128 * NTOK, NPC], [1, CT]])
                cLX.inc(sy.dma_start(out=X32[:, :], in_=src), 16)
            n_x = cLX.n

            xb_pe = [0, 0]
            xsq_pe = [0, 0]
            xb_ac = [0, 0]
            xsq_dv = [0, 0]
            for pc in range(NPC):
                par = pc % 2

                @block.gpsimd
                def _(g, pc=pc, par=par, n_x=n_x, mufree=xb_pe[par],
                      sqfree=xsq_dv[par]):
                    if pc == 0:
                        g.wait_ge(cLX.sem, n_x)
                    if pc >= 2:
                        cPE.wait(g, mufree)
                        cDV.wait(g, sqfree)
                    cPL.inc(g.tensor_copy(XB[:, par * 512:(par + 1) * 512],
                                          X32[:, pc * 512:(pc + 1) * 512]))
                xb_ac[par] = cPL.n

                @block.tensor
                def _(te, pc=pc, par=par, nxb=xb_ac[par]):
                    cPL.wait(te, nxb)
                    if pc == 0:
                        cLW.wait(te)
                    cPE.inc(te.matmul(PB0[:, :], CST["onesl"][:, :],
                                      XB[:, par * 512:(par + 1) * 512],
                                      start=(pc == 0), stop=(pc == NPC - 1)))
                xb_pe[par] = cPE.n
                n_mu = cPE.n

                @block.vector
                def _(v, pc=pc, par=par, nxb=xb_ac[par], sqf=xsq_pe[par]):
                    cPL.wait(v, nxb)
                    if pc >= 2:
                        cPE.wait(v, sqf)
                    cDV.inc(v.tensor_mul(XSQ[:, par * 512:(par + 1) * 512],
                                         XB[:, par * 512:(par + 1) * 512],
                                         XB[:, par * 512:(par + 1) * 512]))
                xsq_dv[par] = cDV.n

                @block.tensor
                def _(te, pc=pc, par=par, nsq=xsq_dv[par]):
                    cDV.wait(te, nsq)
                    cPE.inc(te.matmul(PB1[:, :], CST["onesl"][:, :],
                                      XSQ[:, par * 512:(par + 1) * 512],
                                      start=(pc == 0), stop=(pc == NPC - 1)))
                xsq_pe[par] = cPE.n
            n_msq = cPE.n

            @block.scalar
            def _(s, n_mu=n_mu):
                cPE.wait(s, n_mu)
                cAC.inc(s.activation(MU2[:, :], PB0[:, :], AF.Square))

            @block.vector
            def _(v, n_msq=n_msq):
                cAC.wait(v)
                cPE.wait(v, n_msq)
                cDV.inc(v.tensor_sub(VART[:, :], PB1[:, :], MU2[:, :]))

            @block.scalar
            def _(s):
                cDV.wait(s)
                cAC.inc(s.activation(VART[:, :], VART[:, :], AF.Sqrt,
                                     bias=pv("eps5")))
            n_srt = cAC.n

            @block.vector
            def _(v, n_srt=n_srt):
                cAC.wait(v, n_srt)
                cDV.inc(v.reciprocal(RREP[:, :], VART[:, :]))
            n_rrep = cDV.n

            g_pl = [0, 0]
            for pc in range(NPC):
                par = pc % 2

                @block.vector
                def _(v, pc=pc, par=par, n_rrep=n_rrep, ttfree=g_pl[par]):
                    if pc == 0:
                        cDV.wait(v, n_rrep)
                    if pc >= 2:
                        cPL.wait(v, ttfree)
                    cDV.inc(v.tensor_sub(TT[:, par * 512:(par + 1) * 512],
                                         X32[:, pc * 512:(pc + 1) * 512], PB0[:, :]))

                @block.gpsimd
                def _(g, pc=pc, par=par, t0=t0):
                    cDV.wait(g)
                    cPL.inc(g.tensor_tensor(XF[pc][:, 1 + t0:1 + t0 + CT],
                                            TT[:, par * 512:(par + 1) * 512],
                                            RREP[:, :], ALU.mult))
                g_pl[par] = cPL.n

                @block.gpsimd
                def _(g, pc=pc, t0=t0):
                    cPL.wait(g)
                    cPL.inc(g.tensor_copy(XNB[:, pc * 512:(pc + 1) * 512],
                                          XF[pc][:, 1 + t0:1 + t0 + CT]))
            n_xnb = cPL.n

            if A_SEC is not None and A_SEC <= 1:
                continue
            qkv_ac = [0, 0]
            for halfq in range(2):
                @block.sync
                def _(sy, halfq=halfq):
                    cPE.wait(sy)
                    src = AP(wb_h["awqkv"].tensor, halfq * 1536,
                             [[3072, 128], [128 * 3072, NPC], [1, 1536]])
                    cLQ.inc(sy.dma_start(
                        out=AWQKV[:, :].rearrange("p (a b) -> p a b", a=NPC),
                        in_=src), 16)
                n_awq = cLQ.n

                for ofl in range(12):
                    ofc = halfq * 12 + ofl
                    pbank = PB0 if ofc % 2 == 0 else PB1

                    @block.tensor
                    def _(te, ofl=ofl, ofc=ofc, pbank=pbank, n_awq=n_awq,
                          bankfree=qkv_ac[ofc % 2]):
                        if ofl == 0:
                            cPL.wait(te, n_xnb)
                            te.wait_ge(cLQ.sem, n_awq)
                            cLW.wait(te)
                        cDV.wait(te, bankfree)
                        for kc in range(NPC):
                            cPE.inc(te.matmul(
                                pbank[:, :],
                                AWQKV[:, kc * 1536 + ofl * 128:kc * 1536 + ofl * 128 + 128],
                                XNB[:, kc * 512:(kc + 1) * 512],
                                start=(kc == 0), stop=(kc == NPC - 1)))

                    @block.vector
                    def _(v, ofc=ofc, pbank=pbank):
                        cPE.wait(v)
                        if ofc < 8:
                            cDV.inc(v.tensor_scalar(
                                QB[:, ofc * 512:(ofc + 1) * 512],
                                pbank[:, :], 0.125, None, ALU.mult))
                        elif ofc < 16:
                            cDV.inc(v.tensor_copy(
                                KB[:, (ofc - 8) * 512:(ofc - 7) * 512],
                                pbank[:, :]))
                        else:
                            cDV.inc(v.tensor_copy(VT[:, :], pbank[:, :]))
                    qkv_ac[ofc % 2] = cDV.n

                    if ofc >= 16:
                        pcv = ofc - 16

                        @block.tensor
                        def _(te):
                            cDV.wait(te)
                            for tb in range(4):
                                cPE.inc(te.transpose(PTB[:, tb * 128:(tb + 1) * 128],
                                                     VT[:, tb * 128:(tb + 1) * 128],
                                                     CST["ident"][:, :]))

                        @block.vector
                        def _(v, pcv=pcv):
                            cPE.wait(v)
                            base = VTM[:, :]
                            dst = AP(base.tensor, base.offset + pcv * 128,
                                     [[4096, 128], [1024, 4], [1, 128]])
                            cDV.inc(v.tensor_copy(
                                dst, PTB[:, 0:512].rearrange("p (a b) -> p a b", a=4)))

            if A_SEC is not None and A_SEC <= 2:
                continue
            for pc in range(NPC):
                @block.tensor
                def _(te, pc=pc):
                    cDV.wait(te)
                    for tb in range(4):
                        cPE.inc(te.transpose(
                            PTB[:, tb * 128:(tb + 1) * 128],
                            KB[:, pc * 512 + tb * 128:pc * 512 + (tb + 1) * 128],
                            CST["ident"][:, :]))

                @block.vector
                def _(v):
                    cPE.wait(v)
                    cDV.inc(v.tensor_copy(KTMt[:, :], PTB[:, 0:512]))

                @block.tensor
                def _(te, pc=pc):
                    cDV.wait(te)
                    for tb in range(4):
                        cPE.inc(te.matmul(PGX[:, pc * 9:pc * 9 + 9],
                                          KTMt[:, tb * 128:(tb + 1) * 128],
                                          CST["cumloc"][:, tb * 9:(tb + 1) * 9],
                                          start=(tb == 0), stop=(tb == 3)))
                    for tb in range(4):
                        cPE.inc(te.matmul(
                            PGX[:, 128 + pc * 9:128 + pc * 9 + 9],
                            VTM[:, tb * 1024 + pc * 128:tb * 1024 + (pc + 1) * 128],
                            CST["cumloc"][:, tb * 9:(tb + 1) * 9],
                            start=(tb == 0), stop=(tb == 3)))
            n_ckmm = cPE.n

            for pc in range(NPC):
                par = pc % 2

                @block.vector
                def _(v, pc=pc, par=par, n_ckmm=n_ckmm):
                    if pc == 0:
                        cPE.wait(v, n_ckmm)
                    cDV.wait(v)
                    cDV.inc(v.tensor_scalar(CK9[:, par * 9:(par + 1) * 9],
                                            PGX[:, pc * 9:pc * 9 + 9],
                                            CARRYK[:, pc:pc + 1], None, ALU.add))
                    cDV.wait(v)
                    cDV.inc(v.tensor_copy(CARRYK[:, pc:pc + 1],
                                          CK9[:, par * 9 + 8:par * 9 + 9]))
                    cDV.inc(v.tensor_copy(CKB[:, pc * 8:(pc + 1) * 8],
                                          CK9[:, par * 9:par * 9 + 8]))
            for pc in range(NPC):
                par = pc % 2

                @block.vector
                def _(v, pc=pc, par=par):
                    cDV.wait(v)
                    cDV.inc(v.tensor_scalar(CK9[:, par * 9:(par + 1) * 9],
                                            PGX[:, 128 + pc * 9:128 + pc * 9 + 9],
                                            CARRYV[:, pc:pc + 1], None, ALU.add))
                    cDV.wait(v)
                    cDV.inc(v.tensor_copy(CARRYV[:, pc:pc + 1],
                                          CK9[:, par * 9 + 8:par * 9 + 9]))
                    cDV.inc(v.tensor_copy(CVB[:, pc * 8:(pc + 1) * 8],
                                          CK9[:, par * 9:par * 9 + 8]))
            n_ckcv = cDV.n

            @block.tensor
            def _(te, n_ckcv=n_ckcv):
                cDV.wait(te, n_ckcv)
                cDV.wait(te)
                for pc in range(NPC):
                    cPE.inc(te.transpose(PTB[0:8, pc * 128:(pc + 1) * 128],
                                         CVB[:, pc * 8:(pc + 1) * 8],
                                         CST["ident"][:, :]))

            @block.vector
            def _(v):
                cPE.wait(v)
                cDV.inc(v.tensor_copy(CVT[0:8, :], PTB[0:8, 0:1024]))

            if A_SEC is not None and A_SEC <= 3:
                continue
            for p in range(4):
                for half in range(2):
                    pob = PO0 if half == 0 else PO1

                    @block.tensor
                    def _(te, p=p, half=half):
                        cAC.wait(te)
                        cDV.wait(te)
                        for hh in range(2):
                            cPE.inc(te.transpose(PTB[:, 0:128], XNB[:, 0:128],
                                                 CST["ident"][:, :]))
                            for hpl in range(4):
                                hp = half * 4 + hpl
                                bank = PS0 if hpl < 2 else PS1
                                slot = (hpl % 2) * 2 + hh
                                qsl = QB[hh * 64:hh * 64 + 64,
                                         hp * 512 + p * 128:hp * 512 + (p + 1) * 128]
                                ksl = KB[hh * 64:hh * 64 + 64,
                                         hp * 512 + p * 128:hp * 512 + (p + 1) * 128]
                                cPE.inc(te.matmul(
                                    bank[:, slot * 128:(slot + 1) * 128],
                                    qsl, ksl, start=True, stop=True))
                    n_s = cPE.n
                    if A_PAIR is not None and A_PAIR <= 1:
                        continue

                    @block.vector
                    def _(v, n_s=n_s):
                        cPE.wait(v, n_s)
                        cDV.inc(v.tensor_tensor(PS0[:, :], PS0[:, :],
                                                CST["amsk4"][:, :], ALU.add))
                        cDV.inc(v.tensor_tensor(PS1[:, :], PS1[:, :],
                                                CST["amsk4"][:, :], ALU.add))
                    n_madd = cDV.n

                    @block.scalar
                    def _(s, n_madd=n_madd):
                        cDV.wait(s, n_madd)
                        cAC.inc(s.activation(PEXP[:, 0:512], PS0[:, :], AF.Exp))
                        cAC.inc(s.activation(PEXP[:, 512:1024], PS1[:, :],
                                             AF.Exp))
                    n_exp = cAC.n
                    if A_PAIR is not None and A_PAIR <= 2:
                        continue

                    @block.vector
                    def _(v, n_exp=n_exp):
                        cAC.wait(v, n_exp)
                        cDV.inc(v.reduce_sum(
                            SSUM[:, 0:8],
                            PEXP[:, :].rearrange("p (c j) -> p c j", c=8),
                            axis=AX.X))
                        cDV.wait(v)
                        cDV.inc(v.reciprocal(SREC[:, 0:8], SSUM[:, 0:8]))
                    n_rec = cDV.n
                    if A_PAIR is not None and A_PAIR <= 3:
                        continue

                    @block.vector
                    def _(v, n_rec=n_rec):
                        cDV.wait(v, n_rec)
                        for sl in range(8):
                            cDV.inc(v.tensor_scalar(
                                PEXP[:, sl * 128:(sl + 1) * 128],
                                PEXP[:, sl * 128:(sl + 1) * 128],
                                SREC[:, sl:sl + 1], None, ALU.mult))
                    n_norm = cDV.n
                    if A_PAIR is not None and A_PAIR <= 4:
                        continue

                    @block.tensor
                    def _(te, n_norm=n_norm):
                        cDV.wait(te, n_norm)
                        for sl in range(8):
                            cPE.inc(te.transpose(PTB[:, sl * 128:(sl + 1) * 128],
                                                 PEXP[:, sl * 128:(sl + 1) * 128],
                                                 CST["ident"][:, :]))

                    @block.vector
                    def _(v):
                        cPE.wait(v)
                        cDV.inc(v.tensor_copy(PTS[:, :], PTB[:, :]))
                    n_pts = cDV.n
                    if A_PAIR is not None and A_PAIR <= 5:
                        continue

                    @block.tensor
                    def _(te, p=p, half=half):
                        cDV.wait(te)
                        for hh in range(2):
                            cPE.inc(te.transpose(PTB[:, 0:128], XNB[:, 0:128],
                                                 CST["ident"][:, :]))
                            for hpl in range(4):
                                hp = half * 4 + hpl
                                slot = hpl * 2 + hh
                                qsl = QB[hh * 64:hh * 64 + 64,
                                         hp * 512 + p * 128:hp * 512 + (p + 1) * 128]
                                cksl = CKB[hh * 64:hh * 64 + 64, hp * 8:hp * 8 + 8]
                                cPE.inc(te.matmul(PGX[:, slot * 8:(slot + 1) * 8],
                                                  qsl, cksl, start=True,
                                                  stop=True))
                    n_g = cPE.n

                    @block.scalar
                    def _(s, n_g=n_g):
                        cPE.wait(s, n_g)
                        cAC.inc(s.activation(EG[:, :], PGX[:, 0:64], AF.Exp,
                                             scale=-1.0))

                    @block.vector
                    def _(v):
                        cAC.wait(v)
                        cDV.inc(v.tensor_scalar(EG[:, :], EG[:, :], 1.0, None,
                                                ALU.add))
                        cDV.wait(v)
                        cDV.inc(v.reciprocal(SG[:, :], EG[:, :]))

                    @block.gpsimd
                    def _(g, p=p):
                        cDV.wait(g)
                        cPL.inc(g.tensor_tensor(TS_[:, :], SG[:, :],
                                                CST["selt8"][:, p * 64:(p + 1) * 64],
                                                ALU.mult))
                    n_ts = cPL.n

                    @block.tensor
                    def _(te, n_ts=n_ts):
                        cPL.wait(te, n_ts)
                        cDV.wait(te)
                        for sl in range(8):
                            cPE.inc(te.transpose(PTB[0:8, sl * 128:(sl + 1) * 128],
                                                 TS_[:, sl * 8:(sl + 1) * 8],
                                                 CST["ident"][:, :]))

                    @block.vector
                    def _(v):
                        cPE.wait(v)
                        cDV.inc(v.tensor_copy(TST[0:8, :], PTB[0:8, :]))
                    n_tst = cDV.n
                    if A_PAIR is not None and A_PAIR <= 6:
                        continue

                    @block.tensor
                    def _(te, p=p, half=half, pob=pob, n_tst=n_tst, n_pts=n_pts):
                        cDV.wait(te, n_tst)
                        for hpl in range(4):
                            hp = half * 4 + hpl
                            for hh in range(2):
                                slot = hpl * 2 + hh
                                vsl = VTM[:, p * 1024 + hp * 128 + hh * 64:
                                          p * 1024 + hp * 128 + hh * 64 + 64]
                                cPE.inc(te.matmul(
                                    pob[hh * 64:hh * 64 + 64,
                                        hpl * 128:(hpl + 1) * 128],
                                    vsl, PTS[:, slot * 128:(slot + 1) * 128],
                                    start=True, stop=False))
                                cvsl = CVT[0:8, hp * 128 + hh * 64:
                                           hp * 128 + hh * 64 + 64]
                                cPE.inc(te.matmul(
                                    pob[hh * 64:hh * 64 + 64,
                                        hpl * 128:(hpl + 1) * 128],
                                    cvsl, TST[0:8, slot * 128:(slot + 1) * 128],
                                    start=False, stop=True))
                    n_cross = cPE.n

                    @block.vector
                    def _(v, p=p, half=half, pob=pob, n_cross=n_cross):
                        cPE.wait(v, n_cross)
                        base = OB[:, :]
                        dst = AP(base.tensor, base.offset + half * 4 * 512 + p * 128,
                                 [[4096, 128], [512, 4], [1, 128]])
                        cDV.inc(v.tensor_copy(
                            dst, pob[:, :].rearrange("p (a b) -> p a b", a=4)))

            if A_SEC is not None and A_SEC <= 4:
                continue
            wo_dv = [0, 0]
            for ofc in range(NPC):
                pbank = PB0 if ofc % 2 == 0 else PB1

                @block.tensor
                def _(te, ofc=ofc, pbank=pbank, bankfree=wo_dv[ofc % 2]):
                    if ofc < 2:
                        cAC.wait(te)
                        cDV.wait(te)
                    else:
                        cDV.wait(te, bankfree)
                    for kc in range(NPC):
                        cPE.inc(te.matmul(
                            pbank[:, :],
                            AWOUT[:, kc * 1024 + ofc * 128:kc * 1024 + ofc * 128 + 128],
                            OB[:, kc * 512:(kc + 1) * 512],
                            start=(kc == 0), stop=(kc == NPC - 1)))

                @block.vector
                def _(v, ofc=ofc, pbank=pbank):
                    cPE.wait(v)
                    cDV.inc(v.tensor_add(X32[:, ofc * 512:(ofc + 1) * 512],
                                         pbank[:, :],
                                         X32[:, ofc * 512:(ofc + 1) * 512]))
                wo_dv[ofc % 2] = cDV.n
            n_outt = cDV.n

            @block.sync
            def _(sy, seq=seq, t0=t0, n_outt=n_outt, ct=ct):
                cDV.wait(sy, n_outt)
                dst = AP(out_h.tensor, seq * DM * NTOK + t0,
                         [[NTOK, 128], [128 * NTOK, NPC], [1, CT]])
                cSTp[ct % 2].inc(sy.dma_start(out=dst, in_=X32[:, :]), 16)

        ph.close()

    def emit_attn_sweep(seq, l, MUS, RS):
        ph = contextlib.ExitStack()
        ROT = {k: mksb(ph, k, [128, NTOK], F32)
               for k in ("rotqc", "rotqs", "rotkc", "rotks")}
        FWQKV = mksb(ph, "fwqkv", [128, 8 * 768], BF16)
        FWO = mksb(ph, "fwo", [128, 4 * 1024], BF16)
        XFB = mksb(ph, "xfb", [128, 8 * 513], BF16)
        XSQ2 = mksb(ph, "xsq2", [128, 1026], BF16)
        Q32t = mksb(ph, "q32t", [128, 1024], F32)
        QSQt = mksb(ph, "qsqt", [128, 1024], BF16)
        aq = {"bf_dv": [0, 0], "bf_ac": [0, 0], "qsq_dv": [0, 0],
              "nn2": [0, 0], "ln_ac": [0, 0], "t3_dv": [0, 0],
              "exp_ac": [0, 0], "rot_pl": 0}
        la = {"mx": [0, 0, 0], "exp": [0, 0, 0], "pts2": [0, 0, 0], "obf": 0}
        RNt = mksb(ph, "rnt", [128, 1024], F32)
        T3 = mksb(ph, "t3", [128, 512], F32)
        R1 = mksb(ph, "r1", [128, 512], F32)
        R2 = mksb(ph, "r2", [128, 512], F32)
        RH = mksb(ph, "rh", [128, 512], F32)
        QTB = mksb(ph, "qtb", [128, 4 * 512], BF16)
        KTB = mksb(ph, "ktb", [128, 4 * 1024], BF16)
        VTMR = mksb(ph, "vtmr", [128, 8 * 512], BF16)
        VT = mksb(ph, "vt", [128, 512], BF16)
        MN1 = mksb(ph, "mn1", [128, 12], F32)
        SS2 = mksb(ph, "ss2", [128, 12], F32)
        PE1 = mksb(ph, "pe1", [128, 1536], BF16)
        PE2 = mksb(ph, "pe2", [128, 384], BF16)
        PTS2 = mksb(ph, "pts2", [128, 1920], BF16)
        OBF = mksb(ph, "obf", [128, 4 * 512], BF16)
        MU2 = mksb(ph, "mu2", [128, 512], F32)
        VART = mksb(ph, "vart", [128, 512], F32)
        XFBND = mksb(ph, "xfbnd", [128, 16], BF16)

        @block.sync
        def _(sy, l=l):
            phase_guard(sy)
            conv_wait(sy)
            for k in ("rotqc", "rotqs", "rotkc", "rotks"):
                cLW.inc(sy.dma_start(out=ROT[k][:, :], in_=cst_h[k][:, :]), 16)
            src = AP(wb_h["fwo"].tensor, l * 512 * 1024,
                     [[1024, 128], [128 * 1024, 4], [1, 1024]])
            cLW.inc(sy.dma_start(
                out=FWO[:, :].rearrange("p (a b) -> p a b", a=4),
                in_=src), 16)
        n_fw = cLW.n

        for ct in range(NCT):
            t0 = ct * CT
            h0 = ct % 2

            @block.gpsimd
            def _(g, t0=t0, ct=ct):
                cPE.wait(g)
                cDV.wait(g)
                cPL.wait(g)
                for pc in range(NPC):
                    cPL.inc(g.tensor_copy(XFB[:, pc * 513:(pc + 1) * 513],
                                          XF[pc][:, t0:t0 + 513]))
                cPL.wait(g)
                sv, rd = (ct % 2) * 8, ((ct - 1) % 2) * 8
                for pc in range(NPC):
                    cPL.inc(g.tensor_copy(XFBND[:, sv + pc:sv + pc + 1],
                                          XFB[:, pc * 513 + 512:pc * 513 + 513]))
                if ct > 0:
                    for pc in range(NPC):
                        cPL.inc(g.tensor_copy(XFB[:, pc * 513:pc * 513 + 1],
                                              XFBND[:, rd + pc:rd + pc + 1]))
            n_xfb = cPL.n

            pending_post = [None]
            for halfq in range(2):
                @block.sync
                def _(sy, l=l, halfq=halfq):
                    cPE.wait(sy)
                    src = AP(wb_h["fwqkv"].tensor, l * DM * 1536 + halfq * 768,
                             [[1536, 128], [128 * 1536, NPC], [1, 768]])
                    cLQ.inc(sy.dma_start(
                        out=FWQKV[:, :].rearrange("p (a b) -> p a b", a=NPC),
                        in_=src), 16)
                n_fq = cLQ.n

                for ofl in range(6):
                    ofc = halfq * 6 + ofl
                    par2 = ofc % 2
                    pbank = PB0 if ofc % 2 == 0 else PB1

                    @block.tensor
                    def _(te, ofl=ofl, pbank=pbank, n_fq=n_fq, n_xfb=n_xfb,
                          bfdv=aq["bf_dv"][par2], bfac=aq["bf_ac"][par2]):
                        if ofl == 0:
                            cPL.wait(te, n_xfb)
                            te.wait_ge(cLQ.sem, n_fq)
                            cLW.wait(te)
                        cAC.wait(te, bfac)
                        cDV.wait(te, bfdv)
                        for kc in range(NPC):
                            off = 1 if kc < 4 else 0
                            cPE.inc(te.matmul(
                                pbank[:, :],
                                FWQKV[:, kc * 768 + ofl * 128:kc * 768 + ofl * 128 + 128],
                                XFB[:, kc * 513 + off:kc * 513 + off + 512],
                                start=(kc == 0), stop=(kc == NPC - 1)))
                    n_mm = cPE.n

                    if ofc < 8:
                        isq = ofc < 4
                        pc4 = ofc % 4

                        @block.vector
                        def _(v, pbank=pbank, par2=par2, n_mm=n_mm):
                            cPE.wait(v, n_mm)
                            cDV.inc(v.tensor_copy(
                                Q32t[:, par2 * 512:(par2 + 1) * 512], pbank[:, :]))
                            cDV.wait(v)
                            cDV.inc(v.tensor_mul(
                                QSQt[:, par2 * 512:(par2 + 1) * 512],
                                Q32t[:, par2 * 512:(par2 + 1) * 512],
                                Q32t[:, par2 * 512:(par2 + 1) * 512]))
                        aq["bf_dv"][par2] = cDV.n
                        aq["qsq_dv"][par2] = cDV.n

                        def post(ofc=ofc, isq=isq, pc4=pc4, par2=par2,
                                 n_qsq=aq["qsq_dv"][par2], t0=t0, h0=h0, l=l):
                            psb = PS0 if par2 == 0 else PS1

                            @block.tensor
                            def _(te, lnfree=aq["ln_ac"][par2]):
                                cDV.wait(te, n_qsq)
                                cAC.wait(te, lnfree)
                                cPE.inc(te.matmul(
                                    psb[:, :], CST["bd64"][:, :],
                                    QSQt[:, par2 * 512:(par2 + 1) * 512],
                                    start=True, stop=True))
                            aq["nn2"][par2] = cPE.n
                            n_n2 = cPE.n

                            @block.scalar
                            def _(s, n_n2=n_n2, rntfree=aq["t3_dv"][par2]):
                                cPE.wait(s, n_n2)
                                cDV.wait(s, rntfree)
                                cAC.inc(s.activation(
                                    RNt[:, par2 * 512:(par2 + 1) * 512],
                                    psb[:, :], AF.Sqrt, bias=pv("eps24")))
                            aq["ln_ac"][par2] = cAC.n
                            aq["exp_ac"][par2] = cAC.n

                            @block.vector
                            def _(v, nexp=aq["exp_ac"][par2],
                                  rotfree=aq["rot_pl"]):
                                cAC.wait(v, nexp)
                                cPL.wait(v, rotfree)
                                cDV.wait(v)
                                key = ("qs" if isq else "ks", l, pc4)
                                cDV.inc(v.reciprocal(
                                    RNt[:, par2 * 512:(par2 + 1) * 512],
                                    RNt[:, par2 * 512:(par2 + 1) * 512]))
                                cDV.inc(v.tensor_scalar(
                                    T3[:, :], Q32t[:, par2 * 512:(par2 + 1) * 512],
                                    pv(key), None, ALU.mult))
                                cDV.wait(v)
                                cDV.inc(v.tensor_tensor(
                                    T3[:, :], T3[:, :],
                                    RNt[:, par2 * 512:(par2 + 1) * 512],
                                    ALU.mult))
                            n_t3 = cDV.n
                            aq["t3_dv"][par2] = n_t3

                            cs = ROT["rotqc" if isq else "rotkc"]
                            ss = ROT["rotqs" if isq else "rotks"]

                            @block.gpsimd
                            def _(g, cs=cs, n_t3=n_t3):
                                cDV.wait(g, n_t3)
                                cPL.inc(g.tensor_copy(RH[0:32, :], T3[32:64, :]))
                                cPL.inc(g.tensor_copy(RH[64:96, :], T3[96:128, :]))
                                cPL.inc(g.tensor_tensor(R1[:, :], T3[:, :],
                                                        cs[:, t0:t0 + CT],
                                                        ALU.mult))
                            aq["rot_pl"] = cPL.n

                            @block.vector
                            def _(v):
                                cDV.wait(v)
                                cDV.inc(v.tensor_copy(RH[32:64, :], T3[0:32, :]))
                                cDV.inc(v.tensor_copy(RH[96:128, :], T3[64:96, :]))

                            @block.vector
                            def _(v, rotn=aq["rot_pl"]):
                                cPL.wait(v, rotn)
                                cDV.wait(v)
                                cDV.inc(v.tensor_tensor(R2[:, :], RH[:, :],
                                                        ss[:, t0:t0 + CT],
                                                        ALU.mult))

                            @block.vector
                            def _(v):
                                cDV.wait(v)
                                if isq:
                                    dst = QTB[:, pc4 * 512:(pc4 + 1) * 512]
                                else:
                                    dst = KTB[:, pc4 * 1024 + h0 * 512:
                                              pc4 * 1024 + h0 * 512 + 512]
                                cDV.inc(v.tensor_add(dst, R1[:, :], R2[:, :]))

                        if pending_post[0] is not None:
                            pending_post[0]()
                        pending_post[0] = post
                    else:
                        pcv = ofc - 8
                        if pending_post[0] is not None:
                            pending_post[0]()
                            pending_post[0] = None

                        @block.vector
                        def _(v, pbank=pbank, n_mm=n_mm):
                            cPE.wait(v, n_mm)
                            cDV.inc(v.tensor_copy(VT[:, :], pbank[:, :]))
                        aq["bf_dv"][ofc % 2] = cDV.n

                        @block.tensor
                        def _(te):
                            cDV.wait(te)
                            for tb in range(4):
                                cPE.inc(te.transpose(
                                    PTB[:, tb * 128:(tb + 1) * 128],
                                    VT[:, tb * 128:(tb + 1) * 128],
                                    CST["ident"][:, :]))

                        @block.vector
                        def _(v, pcv=pcv, h0=h0):
                            cPE.wait(v)
                            base = VTMR[:, :]
                            dst = AP(base.tensor,
                                     base.offset + h0 * 4 * 512 + pcv * 128,
                                     [[8 * 512, 128], [512, 4], [1, 128]])
                            cDV.inc(v.tensor_copy(
                                dst, PTB[:, 0:512].rearrange("p (a b) -> p a b",
                                                             a=4)))

            la_q = []
            for h in range(8):
                hp, hh = h // 2, h % 2
                for qc in range(4):
                    u = h * 4 + qc
                    pu = u % 3
                    lo = qc if ct > 0 else max(qc, 4)
                    kbs = list(range(lo, qc + 5))
                    nlive = len(kbs)
                    n_a = min(nlive, 4)
                    use_b = nlive > 4
                    bankA = (PS0, PB0, PO1)[pu]
                    bankB = (PS1, PB1, PGX)[pu]

                    if len(la_q) >= 2:
                        la_q[0]("head")

                    @block.tensor
                    def _(te, hp=hp, hh=hh, qc=qc, lo=lo, kbs=kbs, ct=ct, u=u,
                          bankA=bankA, bankB=bankB,
                          mxfree=la["mx"][pu], expfree=la["exp"][pu]):
                        if u < 3:
                            cAC.wait(te)
                            cDV.wait(te)
                        else:
                            cAC.wait(te, expfree)
                            cDV.wait(te, mxfree)
                        for kb in kbs:
                            sl = kb - lo
                            bank = bankA if sl < 4 else bankB
                            col = (sl % 4) * 128
                            half_kb = ((ct - 1) % 2) if kb < 4 else (ct % 2)
                            ksl = KTB[hh * 64:hh * 64 + 64,
                                      hp * 1024 + half_kb * 512 + (kb % 4) * 128:
                                      hp * 1024 + half_kb * 512 + (kb % 4) * 128 + 128]
                            qsl = QTB[hh * 64:hh * 64 + 64,
                                      hp * 512 + qc * 128:hp * 512 + (qc + 1) * 128]
                            cPE.inc(te.matmul(bank[:, col:col + 128], qsl, ksl,
                                              start=True, stop=True))
                    n_s2 = cPE.n

                    if len(la_q) >= 2:
                        la_q[0]("tail")
                        la_q.pop(0)

                    @block.vector
                    def _(v, n_a=n_a, use_b=use_b, n_s2=n_s2, ct=ct, pu=pu,
                          bankA=bankA, bankB=bankB):
                        cPE.wait(v, n_s2)
                        if ct > 0:
                            cDV.inc(v.tensor_tensor(bankA[:, 0:128],
                                                    bankA[:, 0:128],
                                                    CST["macb"][:, :], ALU.add))
                        if use_b:
                            cDV.inc(v.tensor_tensor(bankB[:, 0:128],
                                                    bankB[:, 0:128],
                                                    CST["mcb"][:, :], ALU.add))
                        else:
                            cDV.inc(v.tensor_tensor(
                                bankA[:, (n_a - 1) * 128:n_a * 128],
                                bankA[:, (n_a - 1) * 128:n_a * 128],
                                CST["mcb"][:, :], ALU.add))
                        cDV.wait(v)
                        if use_b:
                            cDV.inc(v.reduce_max(MN1[:, pu * 4:pu * 4 + 1],
                                                 bankA[:, 0:n_a * 128],
                                                 axis=AX.X, negate=True))
                            cDV.inc(v.reduce_max(MN1[:, pu * 4 + 1:pu * 4 + 2],
                                                 bankB[:, 0:128],
                                                 axis=AX.X, negate=True))
                            cDV.wait(v)
                            cDV.inc(v.tensor_tensor(MN1[:, pu * 4:pu * 4 + 1],
                                                    MN1[:, pu * 4:pu * 4 + 1],
                                                    MN1[:, pu * 4 + 1:pu * 4 + 2],
                                                    ALU.min))
                        else:
                            cDV.inc(v.reduce_max(MN1[:, pu * 4:pu * 4 + 1],
                                                 bankA[:, 0:n_a * 128],
                                                 axis=AX.X, negate=True))
                    n_mx = cDV.n
                    la["mx"][pu] = n_mx

                    @block.scalar
                    def _(s, n_a=n_a, use_b=use_b, n_mx=n_mx, pu=pu,
                          bankA=bankA, bankB=bankB):
                        cDV.wait(s, n_mx)
                        if use_b:
                            cAC.inc(s.activation(
                                PE1[:, pu * 512:pu * 512 + n_a * 128],
                                bankA[:, 0:n_a * 128],
                                AF.Exp, bias=MN1[:, pu * 4:pu * 4 + 1],
                                accum_out=SS2[:, pu * 4:pu * 4 + 1]))
                            cAC.inc(s.activation(
                                PE2[:, pu * 128:pu * 128 + 128],
                                bankB[:, 0:128],
                                AF.Exp, bias=MN1[:, pu * 4:pu * 4 + 1],
                                accum_out=SS2[:, pu * 4 + 1:pu * 4 + 2]))
                        else:
                            cAC.inc(s.activation(
                                PE1[:, pu * 512:pu * 512 + n_a * 128],
                                bankA[:, 0:n_a * 128], AF.Exp,
                                bias=MN1[:, pu * 4:pu * 4 + 1],
                                accum_out=SS2[:, pu * 4:pu * 4 + 1]))
                    n_e2 = cAC.n
                    la["exp"][pu] = n_e2

                    @block.vector
                    def _(v, use_b=use_b, n_e2=n_e2, pu=pu, n_a=n_a):
                        cAC.wait(v, n_e2)
                        if use_b:
                            cDV.inc(v.tensor_add(SS2[:, pu * 4:pu * 4 + 1],
                                                 SS2[:, pu * 4:pu * 4 + 1],
                                                 SS2[:, pu * 4 + 1:pu * 4 + 2]))
                            cDV.wait(v)
                        cDV.inc(v.reciprocal(SS2[:, pu * 4 + 3:pu * 4 + 4],
                                             SS2[:, pu * 4:pu * 4 + 1]))
                        cDV.wait(v)
                        if use_b:
                            cDV.inc(v.tensor_scalar(
                                PE1[:, pu * 512:pu * 512 + n_a * 128],
                                PE1[:, pu * 512:pu * 512 + n_a * 128],
                                SS2[:, pu * 4 + 3:pu * 4 + 4], None, ALU.mult))
                            cDV.inc(v.tensor_scalar(
                                PE2[:, pu * 128:pu * 128 + 128],
                                PE2[:, pu * 128:pu * 128 + 128],
                                SS2[:, pu * 4 + 3:pu * 4 + 4], None, ALU.mult))
                        else:
                            cDV.inc(v.tensor_scalar(
                                PE1[:, pu * 512:pu * 512 + n_a * 128],
                                PE1[:, pu * 512:pu * 512 + n_a * 128],
                                SS2[:, pu * 4 + 3:pu * 4 + 4], None, ALU.mult))
                    n_nm = cDV.n

                    def la_lag(part, u=u, pu=pu, nlive=nlive, n_nm=n_nm, h=h,
                               hh=hh, qc=qc, kbs=kbs, lo=lo, ct=ct, hp=hp):
                        if part == "head":
                            @block.tensor
                            def _(te, ptbfree=la["pts2"][(pu + 2) % 3]):
                                cDV.wait(te, n_nm)
                                cDV.wait(te, ptbfree)
                                for sl in range(nlive):
                                    src_ = PE1[:, pu * 512 + sl * 128:
                                               pu * 512 + sl * 128 + 128]                                         if sl < 4 else                                         PE2[:, pu * 128:pu * 128 + 128]
                                    cPE.inc(te.transpose(
                                        PTB[:, sl * 128:(sl + 1) * 128],
                                        src_, CST["ident"][:, :]))
                            npe_t = cPE.n

                            @block.vector
                            def _(v, npe_t=npe_t):
                                cPE.wait(v, npe_t)
                                cDV.inc(v.tensor_copy(
                                    PTS2[:, pu * 640:pu * 640 + nlive * 128],
                                    PTB[:, 0:nlive * 128]))
                            la["pts2"][pu] = cDV.n
                        else:
                            @block.tensor
                            def _(te, npts2=la["pts2"][pu], obffree=la["obf"]):
                                cDV.wait(te, npts2)
                                if u % 8 == 0:
                                    cDV.wait(te, obffree)
                                for kb in kbs:
                                    sl = kb - lo
                                    half_kb = ((ct - 1) % 2) if kb < 4                                         else (ct % 2)
                                    vslot = half_kb * 4 + (kb % 4)
                                    vsl = VTMR[:, vslot * 512 + h * 64:
                                               vslot * 512 + h * 64 + 64]
                                    cPE.inc(te.matmul(
                                        PO0[hh * 64:hh * 64 + 64,
                                            qc * 128:(qc + 1) * 128],
                                        vsl,
                                        PTS2[:, pu * 640 + sl * 128:
                                             pu * 640 + (sl + 1) * 128],
                                        start=(kb == kbs[0]),
                                        stop=(kb == kbs[-1])))
                            if u % 8 == 7:
                                n_pv = cPE.n

                                @block.vector
                                def _(v, n_pv=n_pv, hp=hp):
                                    cPE.wait(v, n_pv)
                                    cDV.inc(v.tensor_copy(
                                        OBF[:, hp * 512:(hp + 1) * 512],
                                        PO0[:, :]))
                                la["obf"] = cDV.n
                    la_q.append(la_lag)

            while la_q:
                la_q[0]("head")
                la_q[0]("tail")
                la_q.pop(0)

            wo_dv2 = [0, 0]
            for ofc in range(NPC):
                pbank = PB0 if ofc % 2 == 0 else PB1

                @block.tensor
                def _(te, ofc=ofc, pbank=pbank, bankfree=wo_dv2[ofc % 2]):
                    if ofc < 2:
                        cAC.wait(te)
                        cDV.wait(te)
                    else:
                        cDV.wait(te, bankfree)
                    for kc in range(4):
                        cPE.inc(te.matmul(
                            pbank[:, :],
                            FWO[:, kc * 1024 + ofc * 128:kc * 1024 + ofc * 128 + 128],
                            OBF[:, kc * 512:(kc + 1) * 512],
                            start=(kc == 0), stop=(kc == 3)))

                @block.vector
                def _(v, ofc=ofc, pbank=pbank, t0=t0):
                    cPE.wait(v)
                    cDV.inc(v.tensor_add(XF[ofc][:, 1 + t0:1 + t0 + CT],
                                         pbank[:, :],
                                         XF[ofc][:, 1 + t0:1 + t0 + CT]))
                wo_dv2[ofc % 2] = cDV.n
            n_woupd = cDV.n

            @block.gpsimd
            def _(g, t0=t0, n_woupd=n_woupd):
                cDV.wait(g, n_woupd)
                cPE.wait(g)
                for pc in range(NPC):
                    cPL.inc(g.tensor_copy(XFB[:, pc * 513:(pc + 1) * 513],
                                          XF[pc][:, t0:t0 + 513]))
            n_xfb2 = cPL.n

            @block.tensor
            def _(te, n_xfb2=n_xfb2):
                cPL.wait(te, n_xfb2)
                for pc in range(NPC):
                    off = 1 if pc < 4 else 0
                    cPE.inc(te.matmul(PB0[:, :], CST["onesl"][:, :],
                                      XFB[:, pc * 513 + off:pc * 513 + off + 512],
                                      start=(pc == 0), stop=(pc == NPC - 1)))
            n_mu2 = cPE.n

            msq2_pe = [0, 0]
            xsq2_dv = [0, 0]
            for pc in range(NPC):
                par = pc % 2
                off = 1 if pc < 4 else 0

                @block.vector
                def _(v, pc=pc, par=par, t0=t0, sqfree=msq2_pe[par]):
                    if pc >= 2:
                        cPE.wait(v, sqfree)
                    cDV.inc(v.tensor_mul(XSQ2[:, par * 513:par * 513 + 513],
                                         XF[pc][:, t0:t0 + 513],
                                         XF[pc][:, t0:t0 + 513]))
                xsq2_dv[par] = cDV.n

                @block.tensor
                def _(te, pc=pc, par=par, off=off, nsq=xsq2_dv[par]):
                    cDV.wait(te, nsq)
                    cPE.inc(te.matmul(PB1[:, :], CST["onesl"][:, :],
                                      XSQ2[:, par * 513 + off:par * 513 + off + 512],
                                      start=(pc == 0), stop=(pc == NPC - 1)))
                msq2_pe[par] = cPE.n
            n_msq2 = cPE.n

            @block.scalar
            def _(s, n_mu2=n_mu2):
                cPE.wait(s, n_mu2)
                cAC.inc(s.activation(MU2[:, :], PB0[:, :], AF.Square))

            @block.vector
            def _(v, t0=t0, n_msq2=n_msq2):
                cAC.wait(v)
                cPE.wait(v, n_msq2)
                cDV.inc(v.tensor_sub(VART[:, :], PB1[:, :], MU2[:, :]))
                cDV.inc(v.tensor_copy(MUS[:, t0:t0 + CT], PB0[:, :]))

            @block.scalar
            def _(s, t0=t0):
                cDV.wait(s)
                cAC.inc(s.activation(VART[:, :], VART[:, :], AF.Sqrt,
                                     bias=pv("eps5")))
            n_mvs = cAC.n

            @block.vector
            def _(v, t0=t0, n_mvs=n_mvs):
                cAC.wait(v, n_mvs)
                cDV.inc(v.reciprocal(RS[:, t0:t0 + CT], VART[:, :]))

        ph.close()

    def emit_ffn_sweep(seq, l, MUS, RS):
        ph = contextlib.ExitStack()
        W1A = mksb(ph, "w1a", [128, 2 * 4096], BF16)
        W1G = mksb(ph, "w1g", [128, 2 * 4096], BF16)
        W2S = mksb(ph, "w2s", [128, 2 * 4096], BF16)
        HNB = mksb(ph, "hnb", [128, 4096], BF16)
        HM = mksb(ph, "hm", [128, NFC * 512], BF16)
        G32 = mksb(ph, "g32", [128, 1024], F32)
        OUTP = mksb(ph, "outp", [128, 4096], F32)
        T4 = mksb(ph, "t4", [128, 512], F32)

        first_sync = [True]
        g32_n = [0, 0]
        hm_n = [0, 0]
        w1_te_n = [0, 0]
        w2_te_n = [0, 0]

        def sguard(sy):
            if first_sync[0]:
                phase_guard(sy)
                first_sync[0] = False

        for ct in reversed(range(NCT if F_SEC != 0 else 0)):
            t0 = ct * CT
            n_outp = None
            if l == 1:
                @block.sync
                def _(sy, seq=seq, t0=t0):
                    sguard(sy)
                    cPL.wait(sy)
                    cSTp[0].wait(sy)
                    cSTp[1].wait(sy)
                    src = AP(out_h.tensor, seq * DM * NTOK + t0,
                             [[NTOK, 128], [128 * NTOK, NPC], [1, CT]])
                    cOP.inc(sy.dma_start(out=OUTP[:, :], in_=src), 16)
                n_outp = cOP.n

            for pc in range(NPC):
                off = 1 if pc < 4 else 0
                par = pc % 2

                @block.vector
                def _(v, pc=pc, par=par, t0=t0, off=off):
                    cPL.wait(v)
                    cDV.wait(v)
                    if pc == 0:
                        cAC.wait(v)
                    cDV.inc(v.tensor_sub(TT[:, par * 512:(par + 1) * 512],
                                         XF[pc][:, t0 + off:t0 + off + CT],
                                         MUS[:, t0:t0 + CT]))

                @block.vector
                def _(v2, par=par, t0=t0):
                    cDV.wait(v2)
                    cPL.inc(v2.tensor_tensor(TT[:, par * 512:(par + 1) * 512],
                                             TT[:, par * 512:(par + 1) * 512],
                                             RS[:, t0:t0 + CT], ALU.mult))

                @block.vector
                def _(v, pc=pc, par=par, l=l):
                    cPL.wait(v)
                    cDV.inc(v.tensor_scalar(HNB[:, pc * 512:(pc + 1) * 512],
                                            TT[:, par * 512:(par + 1) * 512],
                                            pv(("lng", l, pc)),
                                            pv(("lnb", l, pc)),
                                            ALU.mult, ALU.add))
            n_hnb = cAC.n
            if F_SEC is not None and F_SEC <= 1:
                continue

            fc0 = 0
            for gi, gsz in enumerate(FGROUPS):
                par = gi % 2

                @block.sync
                def _(sy, l=l, fc0=fc0, gsz=gsz, par=par, w1free=w1_te_n[par]):
                    sguard(sy)
                    cPE.wait(sy, w1free)
                    src = AP(wb_h["fw1a"].tensor, l * DM * HIDP + fc0 * 128,
                             [[HIDP, 128], [128 * HIDP, NPC], [1, gsz * 128]])
                    cW1p[par].inc(sy.dma_start(
                        out=W1A[:, par * 4096:par * 4096 + NPC * gsz * 128]
                        .rearrange("p (a b) -> p a b", a=NPC),
                        in_=src), 16)

                @block.gpsimd
                def _(g, l=l, fc0=fc0, gsz=gsz, par=par, w1free=w1_te_n[par],
                      gi=gi, ct=ct):
                    if gi == 0 and ct == NCT - 1:
                        cPE.wait(g)
                        cDV.wait(g)
                        cAC.wait(g)
                    cPE.wait(g, w1free)
                    src = AP(wb_h["fw1g"].tensor, l * DM * HIDP + fc0 * 128,
                             [[HIDP, 128], [128 * HIDP, NPC], [1, gsz * 128]])
                    cW1g[par].inc(g.dma_start(
                        out=W1G[:, par * 4096:par * 4096 + NPC * gsz * 128]
                        .rearrange("p (a b) -> p a b", a=NPC),
                        in_=src), 16)
                n_w1 = cW1p[par].n
                n_w1g = cW1g[par].n

                for fcl in range(gsz):
                    fc = fc0 + fcl
                    parf = fc % 2
                    pa = [PB0, PB1][parf]
                    pg = [PS0, PS1][parf]

                    @block.tensor
                    def _(te, fcl=fcl, gsz=gsz, par=par, pa=pa, pg=pg, n_w1=n_w1,
                          fc=fc, n_hnb=n_hnb, pafree=hm_n[parf],
                          pgfree=g32_n[parf], n_w1g=n_w1g):
                        te.wait_ge(cW1p[par].sem, n_w1)
                        te.wait_ge(cW1g[par].sem, n_w1g)
                        if fc == 0:
                            cAC.wait(te, n_hnb)
                            cDV.wait(te)
                        cAC.wait(te, pgfree)
                        cDV.wait(te, pafree)
                        for kc in range(NPC):
                            cPE.inc(te.matmul(
                                pa[:, :],
                                W1A[:, par * 4096 + kc * gsz * 128 + fcl * 128:
                                    par * 4096 + kc * gsz * 128 + fcl * 128 + 128],
                                HNB[:, kc * 512:(kc + 1) * 512],
                                start=(kc == 0), stop=(kc == NPC - 1)))
                        for kc in range(NPC):
                            cPE.inc(te.matmul(
                                pg[:, :],
                                W1G[:, par * 4096 + kc * gsz * 128 + fcl * 128:
                                    par * 4096 + kc * gsz * 128 + fcl * 128 + 128],
                                HNB[:, kc * 512:(kc + 1) * 512],
                                start=(kc == 0), stop=(kc == NPC - 1)))
                    n_ag = cPE.n
                    w1_te_n[par] = n_ag

                    @block.scalar
                    def _(s, fc=fc, parf=parf, pg=pg, l=l, n_ag=n_ag,
                          g32free=hm_n[parf]):
                        cPE.wait(s, n_ag)
                        cDV.wait(s, g32free)
                        cAC.inc(s.activation(G32[:, parf * 512:(parf + 1) * 512],
                                             pg[:, :], AF.Gelu,
                                             bias=pv(("b1g", l, fc))))
                    g32_n[parf] = cAC.n

                    @block.vector
                    def _(v, fc=fc, parf=parf, pa=pa, l=l, n_ag=n_ag,
                          ng32=g32_n[parf]):
                        cPE.wait(v, n_ag)
                        cAC.wait(v, ng32)
                        cDV.inc(v.scalar_tensor_tensor(
                            HM[:, fc * 512:(fc + 1) * 512], pa[:, :],
                            pv(("b1a", l, fc)),
                            G32[:, parf * 512:(parf + 1) * 512],
                            ALU.add, ALU.mult))
                    hm_n[parf] = cDV.n
                fc0 += gsz
            n_hm = cDV.n
            if F_SEC is not None and F_SEC <= 2:
                continue

            hc0 = 0
            for gi, gsz in enumerate(FGROUPS):
                par = gi % 2

                if par == 0:
                    @block.sync
                    def _(sy, l=l, hc0=hc0, gsz=gsz, w2free=w2_te_n[par]):
                        cPE.wait(sy, w2free)
                        src = AP(wb_h["fw2"].tensor,
                                 l * HIDP * DM + hc0 * 128 * DM,
                                 [[DM, 128], [128 * DM, gsz], [1, DM]])
                        cW2p[0].inc(sy.dma_start(
                            out=W2S[:, 0:gsz * 1024]
                            .rearrange("p (a b) -> p a b", a=gsz),
                            in_=src), 16)
                    n_w2 = cW2p[0].n
                else:
                    @block.gpsimd
                    def _(g, l=l, hc0=hc0, gsz=gsz, w2free=w2_te_n[par]):
                        cPE.wait(g, w2free)
                        src = AP(wb_h["fw2"].tensor,
                                 l * HIDP * DM + hc0 * 128 * DM,
                                 [[DM, 128], [128 * DM, gsz], [1, DM]])
                        cW2g2.inc(g.dma_start(
                            out=W2S[:, 4096:4096 + gsz * 1024]
                            .rearrange("p (a b) -> p a b", a=gsz),
                            in_=src), 16)
                    n_w2 = cW2g2.n

                for hcl in range(gsz):
                    hc = hc0 + hcl

                    @block.tensor
                    def _(te, hc=hc, hcl=hcl, par=par, n_w2=n_w2, n_hm=n_hm):
                        te.wait_ge((cW2p[0] if par == 0 else cW2g2).sem, n_w2)
                        if hc == 0:
                            cDV.wait(te, n_hm)
                            cDV.wait(te)
                        for ofc in range(7):
                            cPE.inc(te.matmul(
                                PW[ofc][:, :],
                                W2S[:, par * 4096 + hcl * 1024 + ofc * 128:
                                    par * 4096 + hcl * 1024 + ofc * 128 + 128],
                                HM[:, hc * 512:(hc + 1) * 512],
                                start=(hc == 0), stop=(hc == NFC - 1)))
                w2_te_n[par] = cPE.n
                hc0 += gsz
            n_w2m = cPE.n

            for ofc in range(7):
                @block.vector
                def _(v, ofc=ofc, l=l, t0=t0, n_w2m=n_w2m):
                    cPE.wait(v, n_w2m)
                    if l == 1 and ofc > 0:
                        cPL.wait(v)
                    if l == 0:
                        cDV.inc(v.scalar_tensor_tensor(
                            XF[ofc][:, 1 + t0:1 + t0 + CT], PW[ofc][:, :],
                            pv(("b2", 0, ofc)), XF[ofc][:, 1 + t0:1 + t0 + CT],
                            ALU.add, ALU.add))
                    else:
                        cDV.inc(v.scalar_tensor_tensor(
                            T4[:, :], PW[ofc][:, :], pv(("b2", 1, ofc)),
                            XF[ofc][:, 1 + t0:1 + t0 + CT], ALU.add, ALU.add))

                if l == 1:
                    @block.gpsimd
                    def _(g, ofc=ofc, n_outp=n_outp):
                        cDV.wait(g)
                        g.wait_ge(cOP.sem, n_outp)
                        cPL.inc(g.tensor_tensor(
                            OUTP[:, ofc * 512:(ofc + 1) * 512], T4[:, :],
                            OUTP[:, ofc * 512:(ofc + 1) * 512], ALU.add))

            if F_SEC is not None and F_SEC <= 3:
                continue
            hc0 = 0
            for gi, gsz in enumerate(FGROUPS):
                par = gi % 2

                @block.gpsimd
                def _(g, l=l, hc0=hc0, gsz=gsz, par=par):
                    cPE.wait(g)
                    src = AP(wb_h["fw2"].tensor,
                             l * HIDP * DM + hc0 * 128 * DM + 7 * 128,
                             [[DM, 128], [128 * DM, gsz], [1, 128]])
                    cW2g[par].inc(g.dma_start(
                        out=W2S[:, par * 4096:par * 4096 + gsz * 128]
                        .rearrange("p (a b) -> p a b", a=gsz),
                        in_=src), 16)
                n_w27 = cW2g[par].n

                @block.tensor
                def _(te, hc0=hc0, gsz=gsz, par=par, n_w27=n_w27):
                    te.wait_ge(cW2g[par].sem, n_w27)
                    if hc0 == 0:
                        cDV.wait(te)
                    for hcl in range(gsz):
                        hc = hc0 + hcl
                        cPE.inc(te.matmul(
                            PW[0][:, :],
                            W2S[:, par * 4096 + hcl * 128:par * 4096 + hcl * 128 + 128],
                            HM[:, hc * 512:(hc + 1) * 512],
                            start=(hc == 0), stop=(hc == NFC - 1)))
                w2_te_n[par] = cPE.n
                hc0 += gsz
            n_w27m = cPE.n

            @block.vector
            def _(v, l=l, t0=t0, n_w27m=n_w27m):
                cPE.wait(v, n_w27m)
                if l == 1:
                    cPL.wait(v)
                if l == 0:
                    cDV.inc(v.scalar_tensor_tensor(
                        XF[7][:, 1 + t0:1 + t0 + CT], PW[0][:, :],
                        pv(("b2", 0, 7)), XF[7][:, 1 + t0:1 + t0 + CT],
                        ALU.add, ALU.add))
                else:
                    cDV.inc(v.scalar_tensor_tensor(
                        T4[:, :], PW[0][:, :], pv(("b2", 1, 7)),
                        XF[7][:, 1 + t0:1 + t0 + CT], ALU.add, ALU.add))

            if l == 1:
                @block.gpsimd
                def _(g, n_outp=n_outp):
                    cDV.wait(g)
                    g.wait_ge(cOP.sem, n_outp)
                    cPL.inc(g.tensor_tensor(OUTP[:, 7 * 512:8 * 512], T4[:, :],
                                            OUTP[:, 7 * 512:8 * 512], ALU.add))
                n_fin = cPL.n

                @block.sync
                def _(sy, seq=seq, t0=t0, n_fin=n_fin, ct=ct):
                    cPL.wait(sy, n_fin)
                    dst = AP(out_h.tensor, seq * DM * NTOK + t0,
                             [[NTOK, 128], [128 * NTOK, NPC], [1, CT]])
                    cSTp[ct % 2].inc(sy.dma_start(out=dst, in_=OUTP[:, :]), 16)

        ph.close()

    phases = ["A", "attn0", "ffn0", "attn1", "ffn1"]
    nseq = 1 if PH_LIMIT is not None else BUILD_SEQS
    plim = PH_LIMIT if PH_LIMIT is not None else 5
    for seq in range(nseq):
        todo = phases[:plim]
        if "A" in todo:
            emit_phase_a(seq)
        for l in (0, 1):
            if f"attn{l}" not in todo and f"ffn{l}" not in todo:
                continue
            lay = contextlib.ExitStack()
            MUS = mksb(lay, "mus", [128, NTOK], F32)
            RS = mksb(lay, "rs", [128, NTOK], F32)
            if f"attn{l}" in todo:
                emit_attn_sweep(seq, l, MUS, RS)
            if f"ffn{l}" in todo:
                emit_ffn_sweep(seq, l, MUS, RS)
            lay.close()

    if PH_LIMIT is not None:
        @block.sync
        def _(sy):
            cDV.wait(sy)
            cAC.wait(sy)
            cPL.wait(sy)
            cPE.wait(sy)
            for pc in range(NPC):
                cSTp[0].inc(sy.dma_start(out=dbg_h[pc * 128:(pc + 1) * 128, :],
                                         in_=XF[pc][:, 1:1 + NTOK]), 16)

    @block.sync
    def _(sy):
        sy.wait_ge(cSTp[0].sem, cSTp[0].n)
        sy.wait_ge(cSTp[1].sem, cSTp[1].n)

    stack.close()
    return nc


def kernel(**inputs):
    x = np.asarray(inputs["L0_attn_x"], dtype=np.float32)
    xt = np.ascontiguousarray(x.transpose(0, 2, 1))
    consts = _host_consts()
    pvec, cmap = _host_pvec(inputs)
    w = _host_weights(inputs)

    nc = build_nc(cmap, pvec.shape[1])

    in_maps = []
    for c in range(NCORES):
        m = {"x": xt[c * NSEQ:(c + 1) * NSEQ]}
        m.update(w)
        m.update(consts)
        m["pvec"] = pvec
        in_maps.append(m)
    res = run_bass_kernel_spmd(nc, in_maps, core_ids=list(range(NCORES)))
    out = np.concatenate([r["out"] for r in res.results], axis=0)
    return np.ascontiguousarray(out.transpose(0, 2, 1))
